# version 01
# baseline (speedup 1.0000x reference)
import numpy as np
import ml_dtypes
import concourse.bass as bass
import concourse.mybir as mybir
from concourse.bass_utils import run_bass_kernel_spmd
from contextlib import ExitStack

F32 = mybir.dt.float32
BF16 = mybir.dt.bfloat16
I32 = mybir.dt.int32
AF = mybir.ActivationFunctionType
ALU = mybir.AluOpType
BF = ml_dtypes.bfloat16

L, H, NL, NST, Q, NCH = 4096, 1024, 4, 32, 128, 32
E, INTER = 16, 512
ALPHA, LIMIT = 1.702, 7.0
PI = np.pi
INV2PI = 1.0 / (2.0 * PI)
SQ2PI = 0.7978845608028654
GC3 = 0.044715

def host_constants():
    c = {}
    j = np.arange(Q)
    f = np.arange(Q + 1)
    ang = PI * np.outer(j, f) / Q
    Fre, Fim = np.cos(ang), -np.sin(ang)
    fwd_re = Fre[:, 0:128].copy()
    fwd_im = Fim[:, 0:128].copy()
    fwd_im[:, 0] = Fre[:, 128]
    c["fwd_re"] = fwd_re.astype(BF)
    c["fwd_im"] = fwd_im.astype(BF)
    c["fwdR_re"] = fwd_re[::-1, :].copy().astype(BF)
    fwdR_im = Fim[::-1, 0:128].copy()
    fwdR_im[:, 0] = Fre[::-1, 128]
    c["fwdR_im"] = fwdR_im.astype(BF)
    i_ = np.arange(Q)
    angi = PI * np.outer(f, i_) / Q
    w2 = np.full(Q + 1, 2.0); w2[0] = 1.0; w2[Q] = 1.0
    Gc = (w2[:, None] * np.cos(angi)) / (2 * Q)
    Gs = (-w2[:, None] * np.sin(angi)) / (2 * Q)
    inv_re = Gc[0:128, ::-1].copy()
    inv_im = Gs[0:128, ::-1].copy()
    inv_im[0, :] = Gc[128, ::-1]
    c["inv_re"] = inv_re.astype(BF)
    c["inv_im"] = inv_im.astype(BF)
    m = (Q - 1 - j).astype(np.float64)
    c["mvec_neg"] = (-m).astype(BF)[None, :]
    c["mvec_pos"] = m.astype(BF)[None, :]
    c["row_quart"] = np.full((1, 128), 0.25, BF)
    c["row_one"] = np.ones((1, 128), BF)
    c["ident"] = np.eye(128, dtype=np.float32)
    sel = np.zeros((128, 32, 32), np.float32)
    for v in range(32):
        sel[:, v, v] = 1.0
    c["lnsel"] = sel.reshape(128, 1024).astype(BF)
    p = np.arange(128)
    par = (p % 2).astype(np.float32)
    c["sgn_scan"] = (1.0 - 2.0 * par)[:, None].astype(np.float32)
    c["nsgn_scan"] = (2.0 * par - 1.0)[:, None].astype(np.float32)
    c["quart_re"] = ((1.0 - par) * 0.25)[:, None].astype(np.float32)
    c["mask_re"] = (1.0 - par)[:, None].astype(np.float32)
    c["mask_im"] = par[:, None].astype(np.float32)
    c["row_one_f"] = np.ones((1, 128), np.float32)
    c["row_quart_f"] = np.full((1, 128), 0.25, np.float32)
    c["e0vec"] = np.concatenate([np.ones((1, 1)), np.zeros((1, 127))],
                                axis=1).astype(BF)
    c["identb"] = np.eye(128).astype(BF)
    return c

_CONSTS = host_constants()

def host_prepare(inputs):
    d = {}
    Wo = np.asarray(inputs["L0_attn_W_out"])
    gorder = np.zeros(H, np.int64)
    for g in range(8):
        gorder[g * 128:g * 128 + 64] = np.arange(64 * g, 64 * g + 64)
        gorder[g * 128 + 64:g * 128 + 128] = 512 + np.arange(64 * g, 64 * g + 64)
    d["w_out_t"] = np.ascontiguousarray(np.transpose(Wo, (0, 2, 1))[:, gorder, :])
    gu = np.asarray(inputs["L0_ffn_gate_up_proj"])
    d["gu"] = np.ascontiguousarray(
        np.concatenate([gu[:, :, 0::2], gu[:, :, 1::2]], axis=2))
    gub = np.asarray(inputs["L0_ffn_gate_up_proj_bias"])
    gubd = np.concatenate([gub[:, 0::2], gub[:, 1::2]], axis=1)
    d["gub"] = np.ascontiguousarray(gubd.reshape(E, 8, 128).transpose(0, 2, 1))
    bo = np.asarray(inputs["L0_attn_b_out"])
    d["b_out"] = np.ascontiguousarray(bo.reshape(NL, 16, 128).transpose(0, 2, 1))
    ch2 = (np.arange(128) // 64)
    nidx = (np.arange(128) % 64) // 2
    cd = np.arange(4)
    pm = np.arange(128)
    chmap = (ch2[:, None, None] * 512 + 4 * pm[None, None, :] + cd[None, :, None])
    def scanp(a):
        a = np.asarray(a)
        if a.ndim == 2:
            r = a[:, chmap]
        else:
            r = a[:, chmap, nidx[:, None, None]]
        return np.ascontiguousarray(r.reshape(NL, 128, 512))
    d["log_dt_s"] = scanp(inputs["L0_attn_log_dt"])
    d["A_re_s"] = scanp(inputs["L0_attn_A_re_log"])
    d["A_im_s"] = scanp(inputs["L0_attn_A_im"])
    d["C_re_s"] = scanp(inputs["L0_attn_C_re"])
    d["C_im_s"] = scanp(inputs["L0_attn_C_im"])
    lw = np.asarray(inputs["L0_attn_ln_w"]).reshape(NL, 8, 128)
    lb = np.asarray(inputs["L0_attn_ln_b"]).reshape(NL, 8, 128)
    wb = np.stack([lw, lb], axis=-1)
    d["lnwb"] = np.ascontiguousarray(wb.transpose(0, 2, 1, 3).reshape(NL, 128, 16))
    ds = np.asarray(inputs["L0_attn_D_skip"])
    dsg = np.zeros((NL, H), np.float32)
    for g in range(8):
        for tl in range(64):
            qq = 4 * (16 * g + tl // 4) + tl % 4
            dsg[:, g * 128 + tl] = ds[:, qq]
            dsg[:, g * 128 + 64 + tl] = ds[:, qq + 512]
    d["dskg"] = np.ascontiguousarray(dsg)
    d["gorder"] = gorder
    return d

class Ctr:
    def __init__(self, h, name):
        self.h, self.name, self.n = h, name, 0

import types as _types

def _freeze(fn):
    clo = getattr(fn, "__closure__", None)
    if not clo:
        return fn
    cells = tuple(_types.CellType(c.cell_contents) for c in clo)
    g = _types.FunctionType(fn.__code__, fn.__globals__, fn.__name__,
                            fn.__defaults__, cells)
    g.__kwdefaults__ = fn.__kwdefaults__
    return g


class Prog:
    ENG = ["sync", "vector", "scalar", "tensor", "gpsimd"]
    def __init__(self, all_ctrs=None, serialize=False):
        self.ops = {e: [] for e in self.ENG}
        self.waited = {e: {} for e in self.ENG}
        self.all_ctrs = all_ctrs or []
        self.serialize = serialize
    ENG_CTR = {}
    def op(self, eng, fn, waits=(), inc=None, k=1):
        fn = _freeze(fn)
        if self.serialize:
            waits = list(waits) + list(self.all_ctrs)
            if inc is None:
                inc = self.ENG_CTR.get(eng)
        wl = []
        for w in waits:
            ctr, val = (w, w.n) if isinstance(w, Ctr) else (w[0], w[1])
            if self.waited[eng].get(ctr.name, -1) < val:
                self.waited[eng][ctr.name] = val
                wl.append((ctr, val))
        if inc is not None:
            inc.n += k
        self.ops[eng].append((wl, fn, inc, k))
    def emit(self, block):
        for eng in self.ENG:
            ops = self.ops[eng]
            if not ops:
                continue
            def body(e, ops=ops):
                for wl, fn, inc, k in ops:
                    for ctr, val in wl:
                        e.wait_ge(ctr.h, val)
                    inst = fn(e)
                    if inc is not None:
                        if inst is None:
                            inst = (e.engine_nop() if hasattr(e, "engine_nop")
                                    else e.nop())
                        inst.then_inc(inc.h, k)
            getattr(block, eng)(body)

def build_program(dbg=(), n_inner=NL, do_ffn=True):
    dbg = set(dbg)
    nc = bass.Bass("TRN2", target_bir_lowering=False, debug=False)

    x_in = nc.dram_tensor("x", [L, H], F32, kind="ExternalInput").ap()
    out_t = nc.dram_tensor("out", [L, H], F32, kind="ExternalOutput").ap()
    pr = {}
    def inp(name, shape, dt=F32):
        pr[name] = nc.dram_tensor(name, shape, dt, kind="ExternalInput").ap()
    inp("log_dt_s", [NL, 128, 512]); inp("A_re_s", [NL, 128, 512])
    inp("A_im_s", [NL, 128, 512]); inp("C_re_s", [NL, 128, 512])
    inp("C_im_s", [NL, 128, 512])
    inp("lnwb", [NL, 128, 16]); inp("dskg", [NL, H])
    inp("w_out_t", [NL, H, 2 * H]); inp("b_out", [NL, 128, 16])
    inp("router", [H, E]); inp("gu", [E, H, 2 * INTER]); inp("gub", [E, 128, 8])
    inp("down", [E, INTER, H]); inp("downb", [E, H])
    cst = {}
    for name, arr in _CONSTS.items():
        dt = BF16 if arr.dtype == BF else F32
        cst[name] = nc.dram_tensor("c_" + name, list(arr.shape), dt,
                                   kind="ExternalInput").ap()
    hbm_h = nc.dram_tensor("s_h", [128, 8 * L], BF16, kind="Internal").ap()
    hbm_x = nc.dram_tensor("s_x", [128, 8 * L], BF16, kind="Internal").ap()
    hbm_z = nc.dram_tensor("s_z", [128, 8 * L], BF16, kind="Internal").ap()
    hbm_y = nc.dram_tensor("s_y", [128, 8 * L], BF16, kind="Internal").ap()
    hbm_ap = nc.dram_tensor("s_ap", [128, 8 * 8192], BF16, kind="Internal").ap()
    hbm_acc = nc.dram_tensor("s_acc", [128, 8 * L], F32, kind="Internal").ap()

    dbg_outs = {}
    def dbgt(name, shape, dt=F32):
        if name in dbg:
            dbg_outs[name] = nc.dram_tensor("dbg_" + name, shape, dt,
                                            kind="ExternalOutput").ap()
            return dbg_outs[name]
        return None

    es = ExitStack()
    with es:
        def S(name, pdim, fdim, dt):
            return es.enter_context(nc.sbuf_tensor(name, [pdim, fdim], dt))

        kc = {}
        for name, arr in _CONSTS.items():
            dt = BF16 if arr.dtype == BF else F32
            kc[name] = S("k_" + name, arr.shape[0], arr.shape[1], dt)
        kbias = S("kbias", 128, 2, F32)

        A1 = S("A1", 128, 32768, BF16)
        A2 = S("A2", 128, 32768, BF16)
        pool = [S("pool%d" % i, 128, 512, F32) for i in range(6)]
        s_state = S("s_state", 128, 512, F32)
        s_sw = S("s_sw", 128, 512, F32)
        s_ta = S("s_ta", 128, 512, F32)
        s_tb = S("s_tb", 128, 512, F32)
        Lt1 = S("Lt1", 128, 512, F32)
        Lt2 = S("Lt2", 128, 512, F32)
        wkvec = S("wkvec", 128, 1024, BF16)
        rlayb = S("rlayb", 128, 2048, BF16)
        rlayf = S("rlayf", 128, 1024, F32)
        rstgb = S("rstgb", 1, 2048, BF16)
        rstgf = S("rstgf", 1, 1024, F32)
        stg_st = S("stg_st", 1, 2048, F32)
        kk = S("kk", 128, 128, BF16)
        khAre = S("khAre", 128, 128, BF16)
        khAim = S("khAim", 128, 128, BF16)
        khBre = S("khBre", 128, 128, BF16)
        khBim = S("khBim", 128, 128, BF16)
        dskrow = S("dskrow", 1, H, BF16)
        st_sum = S("st_sum", 32, 128, F32)
        st_rstd = S("st_rstd", 32, 128, F32)
        st_ns2 = S("st_ns2", 32, 128, F32)
        sq_t = S("sq_t", 128, 1024, BF16)
        lnwb_sb = S("lnwb_sb", 128, 16, F32)
        bo_sb = S("bo_sb", 128, 16, F32)
        bo2_sb = S("bo2_sb", 128, 16, F32)
        gub_sb = S("gub_sb", 128, 8, F32)
        gub2_sb = S("gub2_sb", 128, 8, F32)
        stg_sc = S("stg_sc", 1, 2048, BF16)
        dnb_sb = S("dnb_sb", 16, H, BF16)
        scT = S("scT", 16, 2048, BF16)
        rtr_sb = S("rtr_sb", 128, 128, BF16)
        ps = [es.enter_context(nc.psum_tensor('psb%d' % i, [128, 512], F32)) for i in range(8)]

        DQ = Ctr(es.enter_context(nc.semaphore("dq")), "dq")
        DS = Ctr(es.enter_context(nc.semaphore("ds")), "ds")
        DG = Ctr(es.enter_context(nc.semaphore("dg")), "dg")
        PE = Ctr(es.enter_context(nc.semaphore("pe")), "pe")
        VE = Ctr(es.enter_context(nc.semaphore("ve")), "ve")
        SE = Ctr(es.enter_context(nc.semaphore("se")), "se")
        GP = Ctr(es.enter_context(nc.semaphore("gp")), "gp")
        P = Prog(all_ctrs=[DQ, DS, DG, PE, VE, SE, GP], serialize=True)
        P.ENG_CTR = {"tensor": PE, "vector": VE, "scalar": SE}

        def bfview(t, c0, c1):
            return t[:, c0:c1]
        tabB = lambda: A1[:, 0:8192]
        tabA1 = lambda: A1[:, 8192:16384]
        zblk0 = lambda: A1[:, 16384:20480]
        zblk1 = lambda: A1[:, 20480:24576]
        zg = lambda: A1[:, 24576:28672]
        ztm = lambda: A1[:, 28672:32768]
        sp_re = lambda: A1[:, 0:4096]
        sp_im = lambda: A1[:, 4096:8192]
        spT = lambda: A1[:, 8192:12288]
        ytm = lambda: A1[:, 12288:16384]
        tabA2 = lambda: A1[:, 16384:24576]
        PUr = lambda: A2[:, 0:16384]
        wres = lambda: A2[:, 0:16384]
        yhalf = lambda: A2[:, 16384:32768]
        xch = lambda: A1[:]

        def frac_chain(dst_f32, src_psum_ap, tmp_i32_ap):
            P.op("vector", lambda e: e.tensor_copy(tmp_i32_ap, src_psum_ap))
            P.op("vector", lambda e: e.tensor_sub(dst_f32, src_psum_ap, tmp_i32_ap),
                 inc=VE)

        for name in _CONSTS:
            P.op("sync", lambda e, n=name: e.dma_start(out=kc[n][:], in_=cst[n]),
                 inc=DQ, k=16)
        P.op("vector", lambda e: e.memset(kbias[:], 0.0))
        P.op("vector", lambda e: e.memset(kbias[:, 0:1], 1e-5))
        P.op("vector", lambda e: e.memset(kbias[:, 1:2], float(np.pi / 2)))
        P.op("vector", lambda e: e.memset(s_state[:], 0.0), inc=VE)
        for part in range(4):
            P.op("gpsimd", lambda e, p_=part: e.dma_start(
                out=A2[:, p_ * 8192:(p_ + 1) * 8192].rearrange("p (tt c) -> p tt c", tt=8),
                in_=x_in.rearrange("(tt tp) c -> tp tt c", tp=128)
                    [:, p_ * 8:(p_ + 1) * 8, :]), inc=DG, k=16)
        for tt in range(32):
            P.op("sync", lambda e, tt=tt: e.dma_start_transpose(
                out=xch()[:, tt * 1024:(tt + 1) * 1024].rearrange("p (cb i) -> p cb i", cb=8),
                in_=A2[:, tt * 1024:(tt + 1) * 1024].rearrange("p (cb c) -> p cb c", cb=8)),
                 waits=[DG], inc=DQ, k=16)

        def emit_ln_stats(src_fn):
            nmm = 0
            for tt in range(32):
                for cb in range(8):
                    last = (tt == 31 and cb == 7)
                    P.op("vector", lambda e, tt=tt, cb=cb: e.tensor_mul(
                        sq_t[:, 0:128], src_fn(tt, cb), src_fn(tt, cb)),
                         waits=[PE, DQ, DG], inc=VE)
                    P.op("tensor", lambda e, tt=tt, cb=cb, st=(nmm == 0): e.matmul(
                        ps[6][0:32, 0:128],
                        kc["lnsel"][:].rearrange("p (v c) -> p v c", v=32)[:, tt, :],
                        src_fn(tt, cb), start=st, stop=last), waits=[VE])
                    P.op("tensor", lambda e, tt=tt, cb=cb, st=(nmm == 0): e.matmul(
                        ps[7][0:32, 0:128],
                        kc["lnsel"][:].rearrange("p (v c) -> p v c", v=32)[:, tt, :],
                        sq_t[:, 0:128], start=st, stop=last), inc=PE)
                    nmm += 1
            P.op("vector", lambda e: e.tensor_scalar_mul(st_sum[0:32, :], ps[6][0:32, 0:128], 1.0 / H),
                 waits=[PE])
            P.op("vector", lambda e: e.tensor_mul(st_rstd[0:32, :], st_sum[0:32, :], st_sum[0:32, :]))
            P.op("vector", lambda e: e.scalar_tensor_tensor(
                st_rstd[0:32, :], ps[7][0:32, 0:128], 1.0 / H, st_rstd[0:32, :],
                ALU.mult, ALU.subtract), inc=VE)
            P.op("scalar", lambda e: e.activation(st_rstd[0:32, :], st_rstd[0:32, :],
                                                  AF.Ln, bias=kbias[0:32, 0:1]), waits=[VE])
            P.op("scalar", lambda e: e.activation(st_rstd[0:32, :], st_rstd[0:32, :],
                                                  AF.Exp, scale=-0.5), inc=SE)
            P.op("vector", lambda e: e.scalar_tensor_tensor(
                st_ns2[0:32, :], st_sum[0:32, :], -1.0, st_rstd[0:32, :],
                ALU.mult, ALU.mult), waits=[SE], inc=VE)


        P.op("vector", lambda e: e.memset(pool[5][:], 1.0), inc=VE)

        def emit_ln_bcast32(tt):
            if tt % 8 == 0:
                P.op("sync", lambda e, tt=tt: e.dma_start(
                    out=stg_st[0:1, 0:1024], in_=st_rstd[tt:tt + 8, :]),
                     waits=[VE, PE], inc=DQ, k=16)
                P.op("sync", lambda e, tt=tt: e.dma_start(
                    out=stg_st[0:1, 1024:2048], in_=st_ns2[tt:tt + 8, :]),
                     inc=DQ, k=16)
            o = (tt % 8) * 128
            P.op("tensor", lambda e, o=o: e.matmul(
                ps[4][:, 0:128], pool[5][0:1, 0:128],
                stg_st[0:1, o:o + 128],
                start=True, stop=True), waits=[DQ, VE])
            P.op("tensor", lambda e, o=o: e.matmul(
                ps[5][:, 0:128], pool[5][0:1, 0:128],
                stg_st[0:1, 1024 + o:1024 + o + 128],
                start=True, stop=True), inc=PE)

        def apply_ln_tile(tt, src_ap, dst_ap, use_wb):
            def bc(psb):
                return psb[:, 0:128].unsqueeze(1).broadcast_to([128, 8, 128])
            P.op("vector", lambda e: e.tensor_mul(sq_t[:, 0:1024]
                                                  .rearrange("p (cb i) -> p cb i", cb=8),
                                                  src_ap, bc(ps[4])), waits=[PE])
            P.op("vector", lambda e: e.tensor_add(
                sq_t[:, 0:1024].rearrange("p (cb i) -> p cb i", cb=8),
                sq_t[:, 0:1024].rearrange("p (cb i) -> p cb i", cb=8), bc(ps[5])))
            if use_wb:
                P.op("vector", lambda e: e.tensor_mul(
                    sq_t[:, 0:1024].rearrange("p (cb i) -> p cb i", cb=8),
                    sq_t[:, 0:1024].rearrange("p (cb i) -> p cb i", cb=8),
                    lnwb_sb[:, 0:16].rearrange("p (cb two) -> p cb two", cb=8)
                    [:, :, 0:1].broadcast_to([128, 8, 128])))
                P.op("vector", lambda e: e.tensor_add(
                    dst_ap,
                    sq_t[:, 0:1024].rearrange("p (cb i) -> p cb i", cb=8),
                    lnwb_sb[:, 0:16].rearrange("p (cb two) -> p cb two", cb=8)
                    [:, :, 1:2].broadcast_to([128, 8, 128])), inc=VE)
            else:
                P.op("vector", lambda e: e.tensor_copy(
                    dst_ap, sq_t[:, 0:1024].rearrange("p (cb i) -> p cb i", cb=8)),
                     inc=VE)

        emit_ln_stats(lambda tt, cb: xch()[:, tt * 1024 + cb * 128: tt * 1024 + cb * 128 + 128])
        for tt in range(32):
            emit_ln_bcast32(tt)
            apply_ln_tile(tt,
                          xch()[:, tt * 1024:(tt + 1) * 1024].rearrange("p (cb i) -> p cb i", cb=8),
                          A2[:, tt * 1024:(tt + 1) * 1024].rearrange("p (cb i) -> p cb i", cb=8),
                          use_wb=False)
        for hq in range(4):
            P.op("sync", lambda e, hq=hq: e.dma_start(
                out=hbm_h[:, hq * 8192:(hq + 1) * 8192],
                in_=A2[:, hq * 8192:(hq + 1) * 8192]), waits=[VE], inc=DQ, k=16)
            P.op("scalar", lambda e, hq=hq: e.dma_start(
                out=hbm_x[:, hq * 8192:(hq + 1) * 8192],
                in_=A1[:, hq * 8192:(hq + 1) * 8192]), waits=[VE], inc=DS, k=16)
        d = dbgt("z1", [128, 8 * L], BF16)
        if d is not None:
            for hq in range(4):
                P.op("sync", lambda e, hq=hq, d=d: e.dma_start(
                    out=d.rearrange("p (a b) -> p a b", a=4)[:, hq, :],
                    in_=A2[:, hq * 8192:(hq + 1) * 8192]), inc=DQ, k=16)

        ctx = dict(nc=nc, P=P, es=es, S=S, kc=kc, pr=pr, ps=ps, pool=pool,
                   A1=A1, A2=A2, sq_t=sq_t,
                   sems=dict(DQ=DQ, DS=DS, DG=DG, PE=PE, VE=VE, SE=SE, GP=GP),
                   hbm=dict(h=hbm_h, x=hbm_x, z=hbm_z, y=hbm_y, ap=hbm_ap, acc=hbm_acc),
                   regions=dict(tabB=tabB, tabA1=tabA1, zblk0=zblk0, zblk1=zblk1,
                                zg=zg, ztm=ztm, sp_re=sp_re, sp_im=sp_im, spT=spT,
                                ytm=ytm, tabA2=tabA2, PUr=PUr, wres=wres,
                                yhalf=yhalf, xch=xch),
                   smalls=dict(s_state=s_state, s_sw=s_sw, s_ta=s_ta, s_tb=s_tb,
                               Lt1=Lt1, Lt2=Lt2, wkvec=wkvec, rlayb=rlayb,
                               rlayf=rlayf, rstgb=rstgb, rstgf=rstgf,
                               stg_st=stg_st, kk=kk, khAre=khAre,
                               khAim=khAim, khBre=khBre, khBim=khBim,
                               dskrow=dskrow, st_sum=st_sum, st_rstd=st_rstd,
                               st_ns2=st_ns2, lnwb_sb=lnwb_sb, bo_sb=bo_sb,
                               bo2_sb=bo2_sb, gub_sb=gub_sb, gub2_sb=gub2_sb,
                               stg_sc=stg_sc, scT=scT,
                               dnb_sb=dnb_sb, rtr_sb=rtr_sb,
                               kbias=kbias),
                   helpers=dict(emit_ln_stats=emit_ln_stats,
                                emit_ln_bcast32=emit_ln_bcast32,
                                apply_ln_tile=apply_ln_tile,
                                frac_chain=frac_chain),
                   dbgt=dbgt, x_in=x_in, out_t=out_t, do_ffn=do_ffn)

        emit_layers(ctx, n_inner, do_ffn)

        P.op("gpsimd", lambda e: e.engine_nop(), waits=[DQ, DS, DG, PE, VE, SE, GP])
        block = es.enter_context(nc.Block())
        P.emit(block)
    return nc, dbg_outs


def emit_layers(ctx, n_inner, do_ffn):
    nc, P, kc, pr, ps, pool = ctx["nc"], ctx["P"], ctx["kc"], ctx["pr"], ctx["ps"], ctx["pool"]
    A1, A2, sq_t = ctx["A1"], ctx["A2"], ctx["sq_t"]
    sm = ctx["smalls"]; rg = ctx["regions"]; hb = ctx["hbm"]
    DQ, DS, DG = ctx["sems"]["DQ"], ctx["sems"]["DS"], ctx["sems"]["DG"]
    PE, VE, SE, GP = ctx["sems"]["PE"], ctx["sems"]["VE"], ctx["sems"]["SE"], ctx["sems"]["GP"]
    dbgt = ctx["dbgt"]
    out_t = ctx["out_t"]
    Lt1, Lt2, wkvec = sm["Lt1"], sm["Lt2"], sm["wkvec"]
    rlayb, rlayf = sm["rlayb"], sm["rlayf"]
    rstgb, rstgf = sm["rstgb"], sm["rstgf"]
    kk = sm["kk"]; khAre, khAim = sm["khAre"], sm["khAim"]
    khBre, khBim = sm["khBre"], sm["khBim"]
    dskrow = sm["dskrow"]; lnwb_sb = sm["lnwb_sb"]
    bo_sb, bo2_sb = sm["bo_sb"], sm["bo2_sb"]
    gub_sb, gub2_sb = sm["gub_sb"], sm["gub2_sb"]
    stg_sc, dnb_sb, rtr_sb = sm["stg_sc"], sm["dnb_sb"], sm["rtr_sb"]
    scT = sm["scT"]
    s_state, s_sw, s_ta, s_tb = sm["s_state"], sm["s_sw"], sm["s_ta"], sm["s_tb"]
    stg_st = sm["stg_st"]
    st_sum, st_rstd, st_ns2 = sm["st_sum"], sm["st_rstd"], sm["st_ns2"]
    kbias = sm["kbias"]

    def V(fn, waits=()):
        P.op("vector", fn, waits=waits, inc=VE)
    def Sc(fn, waits=()):
        P.op("scalar", fn, waits=waits, inc=SE)
    def T(fn, waits=(), inc=PE):
        P.op("tensor", fn, waits=waits, inc=inc)
    def Dma(q, fn, waits=()):
        ctr = {"sync": DQ, "scalar": DS, "gpsimd": DG}[q]
        P.op(q, fn, waits=waits, inc=ctr, k=16)

    sqf = sq_t[:, 0:1024].bitcast(F32)

    def layer_smalls(l):
        for i, nm in enumerate(["log_dt_s", "A_re_s", "A_im_s", "C_re_s", "C_im_s"]):
            Dma("sync", lambda e, nm=nm, i=i: e.dma_start(out=pool[i][:], in_=pr[nm][l]),
                waits=[VE, PE, SE, DS, DG])
        Dma("scalar", lambda e: e.dma_start(out=lnwb_sb[:], in_=pr["lnwb"][l]),
            waits=[VE, PE])
        Dma("scalar", lambda e: e.dma_start(out=bo_sb[:], in_=pr["b_out"][l]))
        Dma("gpsimd", lambda e: e.dma_start(out=dskrow[0:1, :],
                                            in_=pr["dskg"][l].unsqueeze(0)),
            waits=[VE, PE])
        V(lambda e: e.tensor_scalar_mul(bo2_sb[:], bo_sb[:], 0.5), waits=[DS])
        X0, X1, P2, P3, P4 = pool[0][:], pool[1][:], pool[2][:], pool[3][:], pool[4][:]
        T1, T2, T3, T4 = pool[5][:], s_ta[:], s_tb[:], sqf
        T5, T6 = Lt1[:], Lt2[:]
        al, be = s_state[:], s_sw[:]
        Sc(lambda e: e.activation(T2, X0, AF.Exp), waits=[DQ])
        Sc(lambda e: e.activation(T3, X1, AF.Exp))
        V(lambda e: e.tensor_mul(al, T2, T3), waits=[SE])
        V(lambda e: e.scalar_tensor_tensor(be, T2, 1.0 / (2 * PI), P2,
                                           ALU.mult, ALU.mult))
        V(lambda e: e.tensor_copy(Lt1[:].bitcast(I32), be))
        V(lambda e: e.tensor_sub(be, be, Lt1[:].bitcast(I32)))
        Sc(lambda e: e.activation(T2, al, AF.Exp, scale=-1.0), waits=[VE])
        Sc(lambda e: e.activation(X0, be, AF.Sin, scale=-2 * PI, bias=kbias[:, 1:2]))
        V(lambda e: e.tensor_mul(T1, T2, X0), waits=[SE])
        Sc(lambda e: e.activation(X0, be, AF.Sin, scale=2 * PI), waits=[VE])
        V(lambda e: e.tensor_mul(X1, T2, X0), waits=[SE])
        V(lambda e: e.tensor_mul(X0, T3, T3))
        V(lambda e: e.tensor_mul(T2, P2, P2))
        V(lambda e: e.tensor_add(X0, X0, T2))
        V(lambda e: e.reciprocal(T2, X0))
        V(lambda e: e.tensor_scalar_add(X0, T1, -1.0))
        V(lambda e: e.tensor_mul(T4, X0, T3))
        V(lambda e: e.tensor_mul(T5, X1, T3))
        V(lambda e: e.tensor_mul(T3, X1, P2))
        V(lambda e: e.tensor_sub(T4, T3, T4))
        V(lambda e: e.tensor_mul(T4, T4, T2))
        V(lambda e: e.tensor_mul(T3, X0, P2))
        V(lambda e: e.tensor_add(T5, T5, T3))
        V(lambda e: e.scalar_tensor_tensor(T5, T5, -1.0, T2, ALU.mult, ALU.mult))
        V(lambda e: e.tensor_mul(T3, P4, T5))
        V(lambda e: e.tensor_mul(X0, P3, T4))
        V(lambda e: e.tensor_sub(X0, X0, T3))
        V(lambda e: e.tensor_mul(T3, P3, T5))
        V(lambda e: e.tensor_mul(T6, P4, T4))
        V(lambda e: e.tensor_add(T6, T6, T3))
        V(lambda e: e.tensor_mul(T3, T6, X1))
        V(lambda e: e.tensor_mul(T4, X0, T1))
        V(lambda e: e.tensor_sub(T4, T4, T3))
        V(lambda e: e.tensor_mul(T3, T6, T1))
        V(lambda e: e.tensor_mul(T5, X0, X1))
        V(lambda e: e.tensor_add(T5, T5, T3))
        V(lambda e: e.tensor_mul(T3, T4, T4))
        V(lambda e: e.tensor_mul(X0, T5, T5))
        V(lambda e: e.tensor_add(X0, X0, T3))
        Sc(lambda e: e.activation(X0, X0, AF.Ln), waits=[VE])
        V(lambda e: e.reciprocal(T3, T4), waits=[SE])
        V(lambda e: e.tensor_mul(T3, T5, T3))
        Sc(lambda e: e.activation(T3, T3, AF.Arctan), waits=[VE])
        V(lambda e: e.tensor_single_scalar(X1, T4, 0.0, ALU.is_lt), waits=[SE])
        V(lambda e: e.tensor_scalar_mul(T3, T3, 1.0 / (4 * PI)))
        V(lambda e: e.scalar_tensor_tensor(X1, X1, 0.25, T3, ALU.mult, ALU.add))
        Sc(lambda e: e.activation(T1, al, AF.Exp, scale=-128.0), waits=[VE])
        V(lambda e: e.tensor_scalar_mul(T3, be, 128.0), waits=[SE])
        i32a = Lt2[:].bitcast(I32)
        V(lambda e: e.tensor_copy(i32a, T3))
        V(lambda e: e.tensor_sub(T4, T3, i32a))
        Sc(lambda e: e.activation(T3, T4, AF.Abs), waits=[VE])
        Sc(lambda e: e.activation(T5, T3, AF.Sin, scale=-2 * PI, bias=kbias[:, 1:2]))
        Sc(lambda e: e.activation(T3, T4, AF.Sin, scale=2 * PI))
        V(lambda e: e.tensor_mul(T5, T1, T5), waits=[SE])
        V(lambda e: e.tensor_mul(T6, T1, T3))
        V(lambda e: e.tensor_scalar_mul(T6, T6, kc["nsgn_scan"][:, 0:1]))
        V(lambda e: e.scalar_tensor_tensor(P2, X0, 0.25, al, ALU.mult, ALU.add))
        V(lambda e: e.tensor_scalar_add(P2, P2, 0.34657359027997264))
        Sc(lambda e: e.activation(P3, P2, AF.Exp), waits=[VE])
        V(lambda e: e.tensor_sub(P2, X1, be))
        i32b = T4.bitcast(I32) if hasattr(T4, "bitcast") else None
        i32v = sq_t[:, 0:1024].bitcast(I32)[:, 0:512]
        V(lambda e: e.tensor_copy(i32v, P2))
        V(lambda e: e.tensor_sub(P4, P2, i32v))
        Sc(lambda e: e.activation(P2, P4, AF.Abs), waits=[VE])
        Sc(lambda e: e.activation(T1, P2, AF.Sin, scale=-2 * PI, bias=kbias[:, 1:2]))
        Sc(lambda e: e.activation(P2, P4, AF.Sin, scale=2 * PI))
        V(lambda e: e.tensor_mul(T1, P3, T1), waits=[SE])
        V(lambda e: e.tensor_mul(P2, P3, P2))
        V(lambda e: e.tensor_scalar_mul(T1, T1, kc["mask_re"][:, 0:1]))
        V(lambda e: e.scalar_tensor_tensor(T1, P2, kc["mask_im"][:, 0:1], T1,
                                           ALU.mult, ALU.add))
        V(lambda e: e.memset(wkvec[:], 0.0))
        V(lambda e: e.tensor_copy(
            wkvec[0:64, :].rearrange("p (pm cd two) -> p pm cd two", pm=128, cd=4)
            [:, :, :, 0],
            T1[0:64, 0:512].rearrange("p (cd pm) -> p pm cd", cd=4)))
        V(lambda e: e.tensor_copy(
            wkvec[64:128, :].rearrange("p (pm cd two) -> p pm cd two", pm=128, cd=4)
            [:, :, :, 1],
            T1[64:128, 0:512].rearrange("p (cd pm) -> p pm cd", cd=4)))
        V(lambda e: e.tensor_scalar_mul(P2, be, kc["sgn_scan"][:, 0:1]))
        V(lambda e: e.tensor_scalar(P3, X1, kc["sgn_scan"][:, 0:1],
                                    kc["quart_re"][:, 0:1], ALU.mult, ALU.add))
        def transpose_qty(srcv, writer):
            for t in range(4):
                T(lambda e, t=t, srcv=srcv: e.transpose(
                    ps[0][:, 0:128], srcv[:, t * 128:(t + 1) * 128], kc["ident"][:]),
                  waits=[VE, SE])
                writer(t)
        def wr_split(qoff):
            def inner(t):
                V(lambda e: e.tensor_copy(T4[:, 0:128], ps[0][:, 0:128]), waits=[PE])
                V(lambda e, t=t: e.tensor_copy(
                    rlayb[:, qoff * 512 + t * 128:qoff * 512 + t * 128 + 128],
                    T4[:, 0:128]))
                V(lambda e, t=t: e.tensor_sub(
                    T4[:, 128:256], T4[:, 0:128],
                    rlayb[:, qoff * 512 + t * 128:qoff * 512 + t * 128 + 128]))
                V(lambda e, t=t: e.tensor_copy(
                    rlayb[:, (qoff + 1) * 512 + t * 128:(qoff + 1) * 512 + t * 128 + 128],
                    T4[:, 128:256]))
            return inner
        def wr_f32(qoff):
            def inner(t):
                V(lambda e, t=t: e.tensor_copy(
                    rlayf[:, qoff * 512 + t * 128:qoff * 512 + t * 128 + 128],
                    ps[0][:, 0:128]), waits=[PE])
            return inner
        transpose_qty(al, wr_split(0))
        transpose_qty(P2, wr_split(2))
        transpose_qty(X0, wr_f32(0))
        transpose_qty(P3, wr_f32(1))
        for nm, tsr, shp, dt_ in [("rlayb", rlayb, [128, 2048], BF16),
                                  ("rlayf", rlayf, [128, 1024], F32),
                                  ("Lt1", Lt1, [128, 512], F32),
                                  ("Lt2", Lt2, [128, 512], F32),
                                  ("wkvec", wkvec, [128, 1024], BF16)]:
            dd = dbgt("sm_%s_l%d" % (nm, l), shp, dt_)
            if dd is not None:
                Dma("sync", lambda e, dd=dd, tsr=tsr: e.dma_start(out=dd, in_=tsr[:]),
                    waits=[VE, PE, SE])
        V(lambda e: e.memset(s_state[:], 0.0))

    def layer_ln(l):
        hslot = A2[:, 16384:24576]
        def hload(qq, waits=()):
            Dma("sync", lambda e, qq=qq: e.dma_start(
                out=hslot, in_=hb["h"][:, qq * 8192:(qq + 1) * 8192]),
                waits=[VE, PE] + list(waits))
        def src_fn(tt, cb):
            return hslot[:, (tt % 8) * 1024 + cb * 128:(tt % 8) * 1024 + cb * 128 + 128]
        nmm = [0]
        for qq in range(4):
            hload(qq)
            for tt8 in range(8):
                tt = qq * 8 + tt8
                for cb in range(8):
                    last = (tt == 31 and cb == 7)
                    st = (nmm[0] == 0)
                    V(lambda e, tt=tt, cb=cb: e.tensor_mul(
                        sq_t[:, 0:128], src_fn(tt, cb), src_fn(tt, cb)),
                      waits=[PE, DQ])
                    T(lambda e, tt=tt, cb=cb, st=st, last=last: e.matmul(
                        ps[6][0:32, 0:128],
                        kc["lnsel"][:].rearrange("p (v c) -> p v c", v=32)[:, tt, :],
                        src_fn(tt, cb), start=st, stop=last), waits=[VE, DQ], inc=None)
                    T(lambda e, tt=tt, cb=cb, st=st, last=last: e.matmul(
                        ps[7][0:32, 0:128],
                        kc["lnsel"][:].rearrange("p (v c) -> p v c", v=32)[:, tt, :],
                        sq_t[:, 0:128], start=st, stop=last))
                    nmm[0] += 1
        V(lambda e: e.tensor_scalar_mul(st_sum[0:32, :], ps[6][0:32, 0:128], 1.0 / H),
          waits=[PE])
        V(lambda e: e.tensor_mul(st_rstd[0:32, :], st_sum[0:32, :], st_sum[0:32, :]))
        V(lambda e: e.scalar_tensor_tensor(
            st_rstd[0:32, :], ps[7][0:32, 0:128], 1.0 / H, st_rstd[0:32, :],
            ALU.mult, ALU.subtract))
        Sc(lambda e: e.activation(st_rstd[0:32, :], st_rstd[0:32, :], AF.Ln,
                                  bias=kbias[0:32, 0:1]), waits=[VE])
        Sc(lambda e: e.activation(st_rstd[0:32, :], st_rstd[0:32, :], AF.Exp, scale=-0.5))
        V(lambda e: e.scalar_tensor_tensor(
            st_ns2[0:32, :], st_sum[0:32, :], -1.0, st_rstd[0:32, :],
            ALU.mult, ALU.mult), waits=[SE])
        for bp in range(4):
            for qq in range(4):
                hload(qq, waits=[DG])
                for tt8 in range(8):
                    tt = qq * 8 + tt8
                    if tt % 8 == 0:
                        Dma("scalar", lambda e, tt=tt: e.dma_start(
                            out=stg_st[0:1, 0:1024], in_=st_rstd[tt:tt + 8, :]),
                            waits=[VE, PE])
                        Dma("scalar", lambda e, tt=tt: e.dma_start(
                            out=stg_st[0:1, 1024:2048], in_=st_ns2[tt:tt + 8, :]))
                    o = (tt % 8) * 128
                    T(lambda e, o=o: e.matmul(
                        ps[4][:, 0:128], kc["row_one_f"][0:1, :],
                        stg_st[0:1, o:o + 128], start=True, stop=True),
                      waits=[DS, VE, DQ], inc=None)
                    T(lambda e, o=o: e.matmul(
                        ps[5][:, 0:128], kc["row_one_f"][0:1, :],
                        stg_st[0:1, 1024 + o:1024 + o + 128], start=True, stop=True))
                    for half, cb in ((0, bp), (1, bp + 4)):
                        zdst = (rg["zblk0"]() if half == 0 else rg["zblk1"]())
                        V(lambda e, tt=tt, cb=cb: e.tensor_mul(
                            sq_t[:, 0:128], src_fn(tt, cb), ps[4][:, 0:128]),
                          waits=[PE, DQ])
                        V(lambda e: e.tensor_add(sq_t[:, 0:128], sq_t[:, 0:128],
                                                 ps[5][:, 0:128]))
                        V(lambda e, cb=cb, zdst=zdst, tt=tt: e.tensor_scalar(
                            zdst[:, tt * 128:tt * 128 + 128], sq_t[:, 0:128],
                            lnwb_sb[:, 2 * cb:2 * cb + 1],
                            lnwb_sb[:, 2 * cb + 1:2 * cb + 2], ALU.mult, ALU.add))
            for half, cb in ((0, bp), (1, bp + 4)):
                zsrc = (rg["zblk0"]() if half == 0 else rg["zblk1"]())
                Dma("gpsimd", lambda e, cb=cb, zsrc=zsrc: e.dma_start(
                    out=hb["z"][:, cb * 4096:(cb + 1) * 4096], in_=zsrc),
                    waits=[VE])

    def zg_load(g):
        bl, half = g // 2, g % 2
        Dma("sync", lambda e, bl=bl, half=half: e.dma_start(
            out=rg["zg"]()[0:64, :],
            in_=hb["z"][half * 64:half * 64 + 64, bl * 4096:(bl + 1) * 4096]),
            waits=[VE, PE, DG])
        Dma("sync", lambda e, bl=bl, half=half: e.dma_start(
            out=rg["zg"]()[64:128, :],
            in_=hb["z"][half * 64:half * 64 + 64, (bl + 4) * 4096:(bl + 5) * 4096]))

    def ztm_build():
        Dma("sync", lambda e: e.dma_start_transpose(
            out=rg["ztm"]()[:].rearrange("p (c i) -> p c i", c=32),
            in_=rg["zg"]()[:].rearrange("p (c i) -> p c i", c=32)), waits=[DQ])

    def btable_build(g):
        tabB = rg["tabB"]()
        fb, i32b = pool[0][:], pool[1][:].bitcast(I32)
        snb = pool[2][:].bitcast(BF16)[:, 0:512]
        mgb = pool[2][:].bitcast(BF16)[:, 512:1024]
        def refill(pm):
            Dma("scalar", lambda e, pm=pm: e.dma_start(
                out=rstgb[0:1, :], in_=rlayb[pm:pm + 1, :]), waits=[PE, VE, SE])
            Dma("scalar", lambda e, pm=pm: e.dma_start(
                out=rstgf[0:1, :], in_=rlayf[pm:pm + 1, :]))
        for pmr in range(16):
            pm = 16 * g + pmr
            refill(pm)
            bo = 0
            fo = 0
            T(lambda e, bo=bo: e.matmul(ps[0][:, :], kc["mvec_pos"][0:1, :],
                                        rstgb[0:1, bo + 1024:bo + 1536],
                                        start=True, stop=False),
              waits=[DS, VE, SE], inc=None)
            T(lambda e, bo=bo: e.matmul(ps[0][:, :], kc["mvec_pos"][0:1, :],
                                        rstgb[0:1, bo + 1536:bo + 2048],
                                        start=False, stop=False), inc=None)
            T(lambda e, fo=fo: e.matmul(ps[0][:, :], kc["row_one_f"][0:1, :],
                                        rstgf[0:1, fo + 512:fo + 1024],
                                        start=False, stop=True))
            V(lambda e: e.tensor_copy(i32b, ps[0][:, :]), waits=[PE])
            V(lambda e: e.tensor_sub(fb, ps[0][:, :], i32b))
            Sc(lambda e, pmr=pmr: e.activation(
                tabB[:, pmr * 512:(pmr + 1) * 512], fb, AF.Sin, scale=2 * PI),
               waits=[VE])
        for pmr in range(16):
            pm = 16 * g + pmr
            refill(pm)
            bo = 0
            fo = 0
            T(lambda e, bo=bo: e.matmul(ps[1][:, :], kc["mvec_neg"][0:1, :],
                                        rstgb[0:1, bo:bo + 512],
                                        start=True, stop=False),
              waits=[DS, VE, SE], inc=None)
            T(lambda e, bo=bo: e.matmul(ps[1][:, :], kc["mvec_neg"][0:1, :],
                                        rstgb[0:1, bo + 512:bo + 1024],
                                        start=False, stop=False), inc=None)
            T(lambda e, fo=fo: e.matmul(ps[1][:, :], kc["row_quart_f"][0:1, :],
                                        rstgf[0:1, fo:fo + 512],
                                        start=False, stop=True))
            Sc(lambda e: e.activation(mgb, ps[1][:, :], AF.Exp), waits=[PE])
            V(lambda e, pmr=pmr: e.scalar_tensor_tensor(
                tabB[:, pmr * 512:(pmr + 1) * 512],
                mgb, float(np.sqrt(2.0)),
                tabB[:, pmr * 512:(pmr + 1) * 512], ALU.mult, ALU.mult), waits=[SE])

    def p_matmuls(g):
        tabB = rg["tabB"](); ztm = rg["ztm"](); PU = rg["PUr"]()
        for bank in range(4):
            pbank = ps[2 + bank]
            for s in range(16):
                tl = bank * 16 + s
                gl = tl
                T(lambda e, tl=tl, s=s, st=True: e.matmul(
                    pbank[0:64, s * 32:(s + 1) * 32],
                    tabB[:, tl * 128:tl * 128 + 64],
                    ztm[:].rearrange("p (c ch) -> p ch c", c=32)[:, tl, :],
                    start=True, stop=True), waits=[VE, SE, DQ], inc=None)
                T(lambda e, tl=tl, s=s: e.matmul(
                    pbank[64:128, s * 32:(s + 1) * 32],
                    tabB[:, tl * 128 + 64:tl * 128 + 128],
                    ztm[:].rearrange("p (c ch) -> p ch c", c=32)[:, 64 + tl, :],
                    start=True, stop=True, tile_position=(0, 64)))
            for pmr in range(4):
                def evac(e, bank=bank, pmr=pmr):
                    base = 16 * g + bank * 4 + pmr
                    return e.tensor_copy(
                        PU[:].rearrange("p (c cd pm) -> p c cd pm", c=32, cd=4)
                        [:, :, :, base].rearrange("p c cd -> p cd c"),
                        ps[2 + bank][:, :].rearrange("p (pm cd c) -> p pm cd c",
                                                     pm=4, cd=4)[:, pmr, :, :])
                V(evac, waits=[PE])

    def aprime(g):
        Dma("sync", lambda e: e.dma_start_transpose(
            out=rg["tabA1"]()[:].rearrange("p (tl j) -> p tl j", tl=64),
            in_=rg["tabB"]()[:].rearrange("p (tl c) -> p tl c", tl=64)),
            waits=[VE, SE, PE])
        Dma("gpsimd", lambda e, g=g: e.dma_start(
            out=hb["ap"][:, g * 8192:(g + 1) * 8192], in_=rg["tabA1"]()),
            waits=[DQ])

    def scan():
        PU = rg["PUr"]()
        shuf_mask = []
        for i in range(16):
            shuf_mask += [2 * i + 1, 2 * i]
        for c in range(NCH):
            pslice = PU[:, c * 512:(c + 1) * 512]
            V(lambda e: e.stream_shuffle(s_sw[:], s_state[:], shuf_mask),
              waits=[DG, PE])
            V(lambda e: e.tensor_mul(s_ta[:], s_state[:], Lt1[:]))
            V(lambda e: e.tensor_mul(s_tb[:], s_sw[:], Lt2[:]))
            V(lambda e: e.tensor_add(s_ta[:], s_ta[:], s_tb[:]))
            V(lambda e, pslice=pslice: e.scalar_tensor_tensor(
                s_tb[:], pslice, kc["sgn_scan"][:, 0:1], s_ta[:],
                ALU.mult, ALU.add))
            V(lambda e, pslice=pslice: e.tensor_copy(pslice, s_state[:]))
            V(lambda e: e.tensor_copy(s_state[:], s_tb[:]))

    def granule2(l, g, dbg_y=None):
        tap = (lambda nm, shp: dbgt(nm, shp, BF16)) if (l == 0 and g == 0) \
            else (lambda nm, shp: None)
        d_spfre = tap("spf_re", [128, 4096]); d_spfim = tap("spf_im", [128, 4096])
        d_kk0 = tap("kk0", [128, 128])
        d_khar = tap("khar", [128, 128]); d_khbr = tap("khbr", [128, 128])
        d_khai = tap("khai", [128, 128])
        d_Yre = tap("Y_re", [128, 4096]); d_Yim = tap("Y_im", [128, 4096])
        def tdma(d, src):
            if d is not None:
                Dma("sync", lambda e, d=d, src=src: e.dma_start(out=d, in_=src),
                    waits=[VE, PE, SE])
        tabA = rg["tabA2"]()
        Dma("scalar", lambda e, g=g: e.dma_start(
            out=tabA, in_=hb["ap"][:, g * 8192:(g + 1) * 8192]),
            waits=[VE, PE, SE])
        zg_load(g)
        ztm_build()
        ztm = rg["ztm"](); sp_re = rg["sp_re"](); sp_im = rg["sp_im"]()
        spT = rg["spT"](); ytm = rg["ytm"]()
        for ct in range(8):
            T(lambda e, ct=ct: e.matmul(
                ps[0][:, :], kc["fwd_re"][:], ztm[:, ct * 512:(ct + 1) * 512],
                start=True, stop=True), waits=[DQ, VE], inc=None)
            T(lambda e, ct=ct: e.matmul(
                ps[1][:, :], kc["fwd_im"][:], ztm[:, ct * 512:(ct + 1) * 512],
                start=True, stop=True))
            V(lambda e, ct=ct: e.tensor_copy(sp_re[:, ct * 512:(ct + 1) * 512],
                                             ps[0][:, :]), waits=[PE])
            V(lambda e, ct=ct: e.tensor_copy(sp_im[:, ct * 512:(ct + 1) * 512],
                                             ps[1][:, :]))
        import kernel as _K2
        tdma(d_spfre, sp_re); tdma(d_spfim, sp_im)
        if getattr(_K2, "SUBSTAGE", "") == "g2a":
            return
        for tl in range(64):
            T(lambda e, tl=tl: e.matmul(
                ps[2][:, 2 * tl:2 * tl + 2], tabA[:, tl * 128:(tl + 1) * 128],
                wkvec[:, (16 * g + tl // 4) * 8 + (tl % 4) * 2:
                      (16 * g + tl // 4) * 8 + (tl % 4) * 2 + 2],
                start=True, stop=True),
              waits=[DS, VE], inc=(PE if tl == 63 else None))
        V(lambda e: e.tensor_copy(
            kk[:].rearrange("p (ab t) -> p ab t", ab=2),
            ps[2][:, 0:128].rearrange("p (t ab) -> p ab t", ab=2)), waits=[PE])
        T(lambda e: e.matmul(ps[3][:, 0:128], kc["fwdR_re"][:], kk[:],
                             start=True, stop=True), waits=[VE], inc=None)
        T(lambda e: e.matmul(ps[4][:, 0:128], kc["fwdR_im"][:], kk[:],
                             start=True, stop=True), inc=None)
        T(lambda e, g=g: e.matmul(ps[2][:, 128:256], kc["row_one"][0:1, :],
                                  dskrow[0:1, g * 128:(g + 1) * 128],
                                  start=True, stop=True), inc=None)
        T(lambda e, g=g: e.matmul(ps[2][:, 256:384], kc["e0vec"][0:1, :],
                                  dskrow[0:1, g * 128:(g + 1) * 128],
                                  start=True, stop=True))
        V(lambda e: e.tensor_copy(khAre[:], ps[3][:, 0:128]), waits=[PE])
        V(lambda e: e.tensor_add(khAre[:], khAre[:], ps[2][:, 128:256]))
        V(lambda e: e.tensor_copy(khAim[:], ps[4][:, 0:128]))
        V(lambda e: e.memset(khAim[0:1, :], 0.0))
        V(lambda e: e.tensor_copy(khBre[:], khAre[:]))
        V(lambda e: e.tensor_copy(khBre[0:1, :], ps[4][0:1, 0:128]))
        V(lambda e: e.tensor_add(khBre[0:1, :], khBre[0:1, :],
                                 ps[2][0:1, 256:384]))
        V(lambda e: e.tensor_copy(khBim[:], khAim[:]))
        tdma(d_kk0, kk[:]); tdma(d_khar, khAre[:])
        tdma(d_khbr, khBre[:]); tdma(d_khai, khAim[:])
        if getattr(_K2, "SUBSTAGE", "") == "g2b":
            return
        def khb(t):
            return t[:].unsqueeze(1).broadcast_to([128, 32, 128])
        spv = lambda t: t[:].rearrange("p (c ch) -> p c ch", c=32)
        V(lambda e: e.tensor_mul(spv(spT), spv(sp_re), khb(khBim)))
        V(lambda e: e.tensor_mul(spv(ztm), spv(sp_re), khb(khAre)))
        V(lambda e: e.tensor_mul(spv(sp_re), spv(sp_im), khb(khAim)))
        V(lambda e: e.tensor_sub(spv(sp_re), spv(ztm), spv(sp_re)))
        V(lambda e: e.tensor_mul(spv(sp_im), spv(sp_im), khb(khBre)))
        V(lambda e: e.tensor_add(spv(sp_im), spv(spT), spv(sp_im)))
        tdma(d_Yre, sp_re); tdma(d_Yim, sp_im)
        if getattr(_K2, "SUBSTAGE", "") == "g2c":
            return
        PU = rg["PUr"]()
        for t8 in range(8):
            bank = ps[5 + (t8 % 2)]
            def mvap(sp, t8=t8):
                a = sp[:].rearrange("p (c hh gs) -> p gs hh c", hh=2, gs=64)
                return a[:, 8 * t8:8 * t8 + 8, :, :]
            T(lambda e, bank=bank, mvap=mvap: e.matmul(
                bank[:, :], kc["inv_re"][:], mvap(sp_re),
                start=True, stop=False), waits=[VE], inc=None)
            T(lambda e, bank=bank, mvap=mvap: e.matmul(
                bank[:, :], kc["inv_im"][:], mvap(sp_im),
                start=False, stop=True), inc=None)
            for s in range(8):
                tl = t8 * 8 + s
                pmq = 16 * g + tl // 4
                cdq = tl % 4
                pos = cdq * 128 + pmq
                T(lambda e, tl=tl, s=s, pos=pos: e.matmul(
                    ps[0][:, s * 64:s * 64 + 32],
                    tabA[:, tl * 128:(tl + 1) * 128][0:64, :],
                    PU[:].rearrange("p (c q) -> p q c", c=32)[0:64, pos, :],
                    start=True, stop=True), inc=None)
                T(lambda e, tl=tl, s=s, pos=pos: e.matmul(
                    ps[1][:, s * 64 + 32:s * 64 + 64],
                    tabA[:, tl * 128:(tl + 1) * 128][64:128, :],
                    PU[:].rearrange("p (c q) -> p q c", c=32)[64:128, pos, :],
                    start=True, stop=True, tile_position=(64, 0)),
                  inc=(PE if s == 7 else None))
            for hh in range(2):
                def yslice(t8=t8, hh=hh):
                    return ytm[:].rearrange("p (c ch) -> p ch c", c=32)[
                        :, hh * 64 + 8 * t8:hh * 64 + 8 * t8 + 8, :]
                def evc(e, bank=bank, hh=hh, yslice=yslice):
                    return e.tensor_copy(
                        yslice(),
                        bank[:, :].rearrange("p (gs hh2 c) -> p gs hh2 c",
                                             gs=8, hh2=2)[:, :, hh, :])
                V(evc, waits=[PE])
                def eva(e, hh=hh, yslice=yslice):
                    return e.tensor_add(
                        yslice(), yslice(),
                        ps[hh][:, :].rearrange("p (gs hh2 c) -> p gs hh2 c",
                                               gs=8, hh2=2)[:, :, hh, :])
                V(eva)
        if dbg_y is not None:
            Dma("sync", lambda e, d=dbg_y, g=g: e.dma_start(
                out=d.rearrange("p (g t) -> p g t", g=8)[:, g, :], in_=ytm),
                waits=[VE])
        if getattr(_K2, "SUBSTAGE", "") == "g2d":
            return
        yv = ytm
        Sc(lambda e: e.activation(spT, yv, AF.Square), waits=[VE])
        V(lambda e: e.tensor_mul(spT, spT, yv), waits=[SE])
        V(lambda e: e.scalar_tensor_tensor(spT, spT, GC3, yv, ALU.mult, ALU.add))
        Sc(lambda e: e.activation(spT, spT, AF.Tanh, scale=SQ2PI), waits=[VE])
        V(lambda e: e.tensor_scalar(spT, spT, 0.5, 0.5, ALU.mult, ALU.add),
          waits=[SE])
        V(lambda e: e.tensor_mul(yv, yv, spT))
        Dma("sync", lambda e: e.dma_start_transpose(
            out=sp_re[:].rearrange("p (c i) -> p c i", c=32),
            in_=ytm[:].rearrange("p (c i) -> p c i", c=32)), waits=[VE])
        Dma("gpsimd", lambda e, g=g: e.dma_start(
            out=hb["y"][:, g * 4096:(g + 1) * 4096], in_=sp_re), waits=[DQ])

    def w_out(l):
        wres = rg["wres"](); yhalf = rg["yhalf"]()
        d_yg = dbgt("ygel_l%d" % l, [128, 8 * 4096], BF16)
        if d_yg is not None:
            Dma("sync", lambda e, d=d_yg: e.dma_start(out=d, in_=hb["y"]),
                waits=[DG, VE, PE])
        Dma("sync", lambda e: e.dma_start(out=A1[:], in_=hb["h"]),
            waits=[VE, PE, SE, DG, DS])
        for k in range(8):
            Dma("gpsimd", lambda e, k=k: e.dma_start(
                out=wres[:, k * 2048:(k + 1) * 2048],
                in_=pr["w_out_t"][l].rearrange("(k p) o -> p k o", p=128)[:, k, :]),
                waits=[VE, PE])
        sigb = pool[0][:].bitcast(BF16)[:, 0:512]
        zpb = pool[1][:].bitcast(BF16)[:, 0:512]
        d_wres = dbgt("wres_l%d" % l, [128, 16384], BF16)
        if d_wres is not None:
            Dma("sync", lambda e, d=d_wres: e.dma_start(out=d, in_=wres),
                waits=[DG, VE, PE])
        d_yhalf = dbgt("yhalf_l%d" % l, [128, 16384], BF16)
        d_zpb = dbgt("zpb_l%d" % l, [128, 512], BF16)
        for th in range(2):
            Dma("scalar", lambda e, th=th: e.dma_start(
                out=yhalf,
                in_=hb["y"][:].rearrange("p (g c i) -> p g c i", g=8, c=32)
                [:, :, 16 * th:16 * th + 16, :]), waits=[VE, PE, DG])
            if d_yhalf is not None and th == 0:
                Dma("sync", lambda e, d=d_yhalf: e.dma_start(out=d, in_=yhalf),
                    waits=[DS])
            for op_ in range(8):
                for oct_ in range(2):
                    for side, bnk in ((0, 0), (1, 2)):
                        ocol = op_ * 128 + side * 1024
                        for tt8 in range(8):
                            ttl = oct_ * 8 + tt8
                            for k in range(8):
                                lastall = (side == 1 and tt8 == 7 and k == 7)
                                T(lambda e, k=k, ocol=ocol, ttl=ttl, bnk=bnk,
                                  tt8=tt8: e.matmul(
                                    ps[bnk + tt8 // 4][:, (tt8 % 4) * 128:
                                                       (tt8 % 4) * 128 + 128],
                                    wres[:, k * 2048 + ocol:k * 2048 + ocol + 128],
                                    yhalf[:, k * 2048 + ttl * 128:k * 2048 + ttl * 128 + 128]
                                    [:, ::-1],
                                    start=(k == 0), stop=(k == 7)),
                                  waits=[DS, DQ, VE, SE],
                                  inc=(PE if lastall else None))
                    for bi in range(2):
                        linb, gateb = ps[0 + bi], ps[2 + bi]
                        Sc(lambda e, linb=linb, gateb=gateb, op_=op_: e.activation(
                            sigb, gateb[:, :], AF.Tanh, scale=0.5,
                            bias=bo2_sb[:, 8 + op_:9 + op_]), waits=[PE])
                        V(lambda e: e.tensor_scalar(sigb, sigb, 0.5, 0.5,
                                                    ALU.mult, ALU.add), waits=[SE])
                        V(lambda e, linb=linb, op_=op_: e.scalar_tensor_tensor(
                            zpb, linb[:, :], bo_sb[:, op_:op_ + 1], sigb,
                            ALU.add, ALU.mult))
                        def resadd(e, bi=bi, oct_=oct_, th=th, op_=op_):
                            tt0 = th * 16 + oct_ * 8 + bi * 4
                            dst = A1[:].rearrange("p (tt cb i) -> p tt cb i",
                                                  tt=32, cb=8)[:, tt0:tt0 + 4, op_, :]
                            src = zpb.rearrange("p (tt i) -> p tt i", tt=4)
                            return e.tensor_add(dst, dst, src)
                        V(resadd)
                        if d_zpb is not None and th == 0 and op_ == 0 \
                                and oct_ == 0 and bi == 0:
                            Dma("sync", lambda e, d=d_zpb: e.dma_start(
                                out=d, in_=zpb), waits=[VE])
        if l == n_inner - 1:
            for hq in range(4):
                Dma("scalar", lambda e, hq=hq: e.dma_start(
                    out=A2[:, hq * 8192:(hq + 1) * 8192],
                    in_=hb["x"][:, hq * 8192:(hq + 1) * 8192]), waits=[VE, PE])
            for hq in range(4):
                V(lambda e, hq=hq: e.tensor_add(
                    A1[:, hq * 8192:(hq + 1) * 8192],
                    A1[:, hq * 8192:(hq + 1) * 8192],
                    A2[:, hq * 8192:(hq + 1) * 8192]), waits=[DS])
            Dma("sync", lambda e: e.dma_start(out=hb["x"], in_=A1[:]), waits=[VE])
        else:
            Dma("sync", lambda e: e.dma_start(out=hb["h"], in_=A1[:]), waits=[VE])

    import kernel as _K
    sub = getattr(_K, "SUBSTAGE", "all")
    for l in range(n_inner):
        layer_smalls(l)
        if sub == "smalls":
            return
        layer_ln(l)
        if sub == "ln":
            return
        d_tabB = dbgt("tabB_l%d" % l, [128, 8192], BF16) if l == 0 else None
        d_PU = dbgt("PU_l%d" % l, [128, 16384], BF16) if l == 0 else None
        d_U = dbgt("U_l%d" % l, [128, 16384], BF16) if l == 0 else None
        d_y = dbgt("y_l%d" % l, [128, 8 * 4096], BF16) if l == 0 else None
        d_ztm = dbgt("ztm_l%d" % l, [128, 4096], BF16) if l == 0 else None
        for g in range(8):
            zg_load(g)
            ztm_build()
            if d_ztm is not None and g == 0:
                Dma("sync", lambda e, d=d_ztm: e.dma_start(out=d, in_=rg["ztm"]()),
                    waits=[DQ, VE])
            btable_build(g)
            p_matmuls(g)
            if d_tabB is not None and g == 0:
                Dma("sync", lambda e, d=d_tabB: e.dma_start(out=d, in_=rg["tabB"]()),
                    waits=[VE, PE, SE])
            if sub == "pug0" and g == 0:
                Dma("sync", lambda e, d=d_PU: e.dma_start(out=d, in_=rg["PUr"]()),
                    waits=[VE, PE])
                return
            aprime(g)
            if sub == "g1one" and g == 0:
                return
        if sub == "g1":
            return
        if d_PU is not None:
            Dma("sync", lambda e, d=d_PU: e.dma_start(out=d, in_=rg["PUr"]()),
                waits=[VE, PE])
        scan()
        if d_U is not None:
            Dma("sync", lambda e, d=d_U: e.dma_start(out=d, in_=rg["PUr"]()),
                waits=[VE, PE])
        if sub == "scan":
            return
        for g in range(8):
            granule2(l, g, dbg_y=d_y)
            if sub.startswith("g2") and sub != "g2" and g == 0:
                return
        if sub == "g2":
            return
        w_out(l)
        d_h = dbgt("h_l%d" % l, [128, 8 * L], BF16)
        if d_h is not None:
            Dma("sync", lambda e, d=d_h: e.dma_start(
                out=d, in_=hb["x"] if l == n_inner - 1 else hb["h"]),
                waits=[DQ, VE])

    if do_ffn:
        emit_ffn(ctx)
    emit_output(ctx)


def emit_ffn(ctx):
    nc, P, kc, pr, ps, pool = ctx["nc"], ctx["P"], ctx["kc"], ctx["pr"], ctx["ps"], ctx["pool"]
    A1, A2, sq_t = ctx["A1"], ctx["A2"], ctx["sq_t"]
    sm = ctx["smalls"]; rg = ctx["regions"]; hb = ctx["hbm"]
    DQ, DS, DG = ctx["sems"]["DQ"], ctx["sems"]["DS"], ctx["sems"]["DG"]
    PE, VE, SE, GP = ctx["sems"]["PE"], ctx["sems"]["VE"], ctx["sems"]["SE"], ctx["sems"]["GP"]
    kbias = sm["kbias"]
    gub_sb, gub2_sb = sm["gub_sb"], sm["gub2_sb"]
    stg_sc, dnb_sb, rtr_sb = sm["stg_sc"], sm["dnb_sb"], sm["rtr_sb"]
    scT = sm["scT"]
    stg_st = sm["stg_st"]
    st_sum, st_rstd, st_ns2 = sm["st_sum"], sm["st_rstd"], sm["st_ns2"]
    lgts = pool[0]
    AXX = mybir.AxisListType.X

    def V(fn, waits=()):
        P.op("vector", fn, waits=waits, inc=VE)
    def Sc(fn, waits=()):
        P.op("scalar", fn, waits=waits, inc=SE)
    def T(fn, waits=(), inc=PE):
        P.op("tensor", fn, waits=waits, inc=inc)
    def Dma(q, fn, waits=()):
        ctr = {"sync": DQ, "scalar": DS, "gpsimd": DG}[q]
        P.op(q, fn, waits=waits, inc=ctr, k=16)

    Dma("sync", lambda e: e.dma_start(out=A1[:], in_=hb["x"]),
        waits=[VE, PE, SE, DS, DG])
    nmm = [0]
    def src_fn(tt, cb):
        return A1[:, tt * 1024 + cb * 128:tt * 1024 + cb * 128 + 128]
    for tt in range(32):
        for cb in range(8):
            last = (tt == 31 and cb == 7)
            st = (nmm[0] == 0)
            V(lambda e, tt=tt, cb=cb: e.tensor_mul(
                sq_t[:, 0:128], src_fn(tt, cb), src_fn(tt, cb)), waits=[PE, DQ])
            T(lambda e, tt=tt, cb=cb, st=st, last=last: e.matmul(
                ps[6][0:32, 0:128],
                kc["lnsel"][:].rearrange("p (v c) -> p v c", v=32)[:, tt, :],
                src_fn(tt, cb), start=st, stop=last), waits=[VE, DQ], inc=None)
            T(lambda e, tt=tt, cb=cb, st=st, last=last: e.matmul(
                ps[7][0:32, 0:128],
                kc["lnsel"][:].rearrange("p (v c) -> p v c", v=32)[:, tt, :],
                sq_t[:, 0:128], start=st, stop=last))
            nmm[0] += 1
    V(lambda e: e.tensor_scalar_mul(st_sum[0:32, :], ps[6][0:32, 0:128], 1.0 / H),
      waits=[PE])
    V(lambda e: e.tensor_mul(st_rstd[0:32, :], st_sum[0:32, :], st_sum[0:32, :]))
    V(lambda e: e.scalar_tensor_tensor(
        st_rstd[0:32, :], ps[7][0:32, 0:128], 1.0 / H, st_rstd[0:32, :],
        ALU.mult, ALU.subtract))
    Sc(lambda e: e.activation(st_rstd[0:32, :], st_rstd[0:32, :], AF.Ln,
                              bias=kbias[0:32, 0:1]), waits=[VE])
    Sc(lambda e: e.activation(st_rstd[0:32, :], st_rstd[0:32, :], AF.Exp, scale=-0.5))
    V(lambda e: e.scalar_tensor_tensor(
        st_ns2[0:32, :], st_sum[0:32, :], -1.0, st_rstd[0:32, :],
        ALU.mult, ALU.mult), waits=[SE])
    for tt in range(32):
        if tt % 8 == 0:
            Dma("scalar", lambda e, tt=tt: e.dma_start(
                out=stg_st[0:1, 0:1024], in_=st_rstd[tt:tt + 8, :]),
                waits=[VE, PE])
            Dma("scalar", lambda e, tt=tt: e.dma_start(
                out=stg_st[0:1, 1024:2048], in_=st_ns2[tt:tt + 8, :]))
        o = (tt % 8) * 128
        T(lambda e, o=o: e.matmul(
            ps[4][:, 0:128], kc["row_one_f"][0:1, :], stg_st[0:1, o:o + 128],
            start=True, stop=True), waits=[DS, VE], inc=None)
        T(lambda e, o=o: e.matmul(
            ps[5][:, 0:128], kc["row_one_f"][0:1, :],
            stg_st[0:1, 1024 + o:1024 + o + 128], start=True, stop=True))
        def bc(psb):
            return psb[:, 0:128].unsqueeze(1).broadcast_to([128, 8, 128])
        V(lambda e, tt=tt: e.tensor_mul(
            sq_t[:, 0:1024].rearrange("p (cb i) -> p cb i", cb=8),
            A1[:, tt * 1024:(tt + 1) * 1024].rearrange("p (cb i) -> p cb i", cb=8),
            bc(ps[4])), waits=[PE])
        V(lambda e, tt=tt: e.tensor_add(
            A1[:, tt * 1024:(tt + 1) * 1024].rearrange("p (cb i) -> p cb i", cb=8),
            sq_t[:, 0:1024].rearrange("p (cb i) -> p cb i", cb=8), bc(ps[5])))
    Dma("sync", lambda e: e.dma_start(out=hb["z"], in_=A1[:]), waits=[VE])
    d = ctx["dbgt"]("z2", [128, 8 * 4096], BF16)
    if d is not None:
        Dma("sync", lambda e, d=d: e.dma_start(out=d, in_=A1[:]), waits=[VE])

    Dma("gpsimd", lambda e: e.dma_start(
        out=rtr_sb[:].rearrange("p (cb e) -> p cb e", cb=8),
        in_=pr["router"].rearrange("(cb p) e -> p cb e", p=128)), waits=[VE, PE])
    Dma("gpsimd", lambda e: e.dma_start(out=dnb_sb[0:16, :], in_=pr["downb"]))
    V(lambda e: e.tensor_scalar_mul(dnb_sb[0:16, :], dnb_sb[0:16, :], ALPHA),
      waits=[DG])

    wgu = lambda: A1[:, 16384:24576]
    wdn4 = lambda: A1[:, 16384:32768]
    screp = lambda: A1[:, 28672:30720]

    for th in range(2):
        Dma("sync", lambda e, th=th: e.dma_start(
            out=A1[:, 0:16384], in_=hb["z"][:, th * 16384:(th + 1) * 16384]),
            waits=[VE, PE, SE, DG, DS])
        for ttl in range(16):
            for cb in range(8):
                T(lambda e, ttl=ttl, cb=cb: e.matmul(
                    ps[7][0:128, 0:16],
                    A1[:, ttl * 1024 + cb * 128:ttl * 1024 + cb * 128 + 128],
                    rtr_sb[:, cb * 16:(cb + 1) * 16],
                    start=(cb == 0), stop=(cb == 7)), waits=[DQ, DS, VE],
                  inc=(PE if cb == 7 else None))
            V(lambda e, ttl=ttl: e.tensor_copy(lgts[:, ttl * 16:(ttl + 1) * 16],
                                               ps[7][0:128, 0:16]), waits=[PE])
        lg3 = lgts[:, 0:256].rearrange("p (t e) -> p t e", t=16)
        m1, m2 = pool[1], pool[2]
        msk3 = pool[3][:, 0:256].rearrange("p (t e) -> p t e", t=16)
        msk23 = pool[4][:, 0:256].rearrange("p (t e) -> p t e", t=16)
        w1v = pool[5]
        def bce(v):
            return v[:, 0:16].unsqueeze(2).broadcast_to([128, 16, 16])
        V(lambda e: e.reduce_max(m1[:, 0:16], lg3, axis=AXX))
        V(lambda e: e.tensor_tensor(msk3, lg3, bce(m1), ALU.is_ge))
        V(lambda e: e.scalar_tensor_tensor(msk23, msk3, -1e30, lg3, ALU.mult, ALU.add))
        V(lambda e: e.reduce_max(m2[:, 0:16], msk23, axis=AXX))
        V(lambda e: e.tensor_sub(w1v[:, 0:16], m1[:, 0:16], m2[:, 0:16]))
        Sc(lambda e: e.activation(w1v[:, 0:16], w1v[:, 0:16], AF.Tanh, scale=0.5),
           waits=[VE])
        V(lambda e: e.tensor_scalar(w1v[:, 0:16], w1v[:, 0:16], 0.5, 0.5,
                                    ALU.mult, ALU.add), waits=[SE])
        V(lambda e: e.tensor_tensor(msk23, msk23, bce(m2), ALU.is_ge))
        scoresv = sq_t[:, 0:256].rearrange("p (t e) -> p t e", t=16)
        V(lambda e: e.tensor_mul(scoresv, msk3, bce(w1v)))
        V(lambda e: e.tensor_scalar(w1v[:, 0:16], w1v[:, 0:16], -1.0, 1.0,
                                    ALU.mult, ALU.add))
        V(lambda e: e.tensor_mul(msk23, msk23, bce(w1v)))
        V(lambda e: e.tensor_add(scoresv, scoresv, msk23))
        for ttl in range(16):
            T(lambda e, ttl=ttl: e.transpose(
                ps[7][:, 0:128].bitcast(BF16)[0:16, 0:128],
                sq_t[:, 0:256].rearrange("p (t e) -> p t e", t=16)[:, ttl, :],
                kc["identb"][:]), waits=[VE])
            V(lambda e, ttl=ttl: e.tensor_scalar_mul(
                scT[0:16, ttl * 128:(ttl + 1) * 128],
                ps[7][:, 0:128].bitcast(BF16)[0:16, 0:128],
                1.0 / ALPHA), waits=[PE])
        d = ctx["dbgt"]("scores_th%d" % th, [128, 256], BF16)
        if d is not None:
            Dma("sync", lambda e, d=d: e.dma_start(out=d, in_=sq_t[:, 0:256]),
                waits=[VE])
        for eg in range(4):
            for ei in range(4):
                e_ = eg * 4 + ei
                Dma("gpsimd", lambda e, e_=e_: e.dma_start(
                    out=wgu().rearrange("p (k f) -> p k f", k=8),
                    in_=pr["gu"][e_].rearrange("(k p) f -> p k f", p=128)),
                    waits=[VE, PE])
                Dma("scalar", lambda e, e_=e_: e.dma_start(
                    out=gub_sb[:, 0:8], in_=pr["gub"][e_]), waits=[VE, PE])
                V(lambda e: e.tensor_scalar_mul(gub2_sb[:, 0:4], gub_sb[:, 0:4],
                                                ALPHA), waits=[DS])
                Dma("sync", lambda e, e_=e_: e.dma_start(
                    out=stg_sc[0:1, :], in_=scT[e_:e_ + 1, :]), waits=[VE, PE])
                for c4 in range(4):
                    T(lambda e, c4=c4: e.matmul(
                        ps[6][:, :], kc["row_one"][0:1, :],
                        stg_sc[0:1, c4 * 512:(c4 + 1) * 512],
                        start=True, stop=True), waits=[DQ, VE], inc=PE)
                    V(lambda e, c4=c4: e.tensor_copy(
                        screp()[:, c4 * 512:(c4 + 1) * 512], ps[6][:, :]),
                      waits=[PE])
                for gp in range(4):
                    for side, bnk in ((0, 0), (1, 4)):
                        gucol = side * 512 + gp * 128
                        for k in range(8):
                            st = (k == 0); last = (k == 7)
                            for t4 in range(4):
                                lastall = (side == 1 and k == 7 and t4 == 3)
                                T(lambda e, k=k, gucol=gucol, t4=t4, bnk=bnk,
                                  st=st, last=last: e.matmul(
                                    ps[bnk + t4][:, :],
                                    wgu()[:, k * 1024 + gucol:k * 1024 + gucol + 128],
                                    A1[:, 0:16384].rearrange(
                                        "p (tt cb i) -> p tt cb i", tt=16, cb=8)
                                    [:, t4 * 4:t4 * 4 + 4, k, :],
                                    start=st, stop=last),
                                  waits=[DG, DQ, VE, SE],
                                  inc=(PE if lastall else None))
                    for t4 in range(4):
                        upb, gtb = ps[4 + t4], ps[0 + t4]
                        up1 = pool[1][:].bitcast(BF16)[:, 0:512]
                        sl1 = pool[2][:].bitcast(BF16)[:, 0:512]
                        Sc(lambda e, upb=upb, gp=gp: e.activation(
                            up1, upb[:, :], AF.Identity,
                            bias=gub_sb[:, 4 + gp:5 + gp]), waits=[PE])
                        Sc(lambda e, gtb=gtb, gp=gp: e.activation(
                            sl1, gtb[:, :], AF.Identity,
                            bias=gub_sb[:, gp:gp + 1]))
                        V(lambda e: e.tensor_scalar_min(sl1, sl1, LIMIT), waits=[SE])
                        Sc(lambda e: e.activation(sl1, sl1, AF.Silu, scale=ALPHA),
                           waits=[VE])
                        V(lambda e: e.tensor_scalar(up1, up1, LIMIT, -LIMIT,
                                                    ALU.min, ALU.max), waits=[SE])
                        V(lambda e: e.tensor_scalar_add(up1, up1, 1.0))
                        V(lambda e: e.tensor_mul(up1, up1, sl1))
                        def actw(e, ei=ei, gp=gp, t4=t4):
                            dst = A2[:].rearrange("p (e2 kq t) -> p e2 kq t",
                                                  e2=4, kq=4)[:, ei, gp,
                                                              t4 * 512:(t4 + 1) * 512]
                            return e.tensor_mul(dst, up1,
                                                screp()[:, t4 * 512:(t4 + 1) * 512])
                        V(actw)
            for ei in range(4):
                e_ = eg * 4 + ei
                Dma("gpsimd", lambda e, e_=e_, ei=ei: e.dma_start(
                    out=wdn4()[:, ei * 4096:(ei + 1) * 4096]
                        .rearrange("p (kq h2) -> p kq h2", kq=4),
                    in_=pr["down"][e_].rearrange("(kq p) h2 -> p kq h2", p=128)),
                    waits=[VE, PE, SE])
            for hc in range(8):
                for t4 in range(4):
                    first = True
                    for ei in range(4):
                        for kq in range(4):
                            st = first; first = False
                            last = (ei == 3 and kq == 3)
                            lastall = (t4 == 3 and last and eg != 0)
                            T(lambda e, hc=hc, t4=t4, kq=kq, ei=ei, st=st,
                              last=last: e.matmul(
                                ps[t4][:, :],
                                wdn4()[:, ei * 4096 + kq * 1024 + hc * 128:
                                       ei * 4096 + kq * 1024 + hc * 128 + 128],
                                A2[:].rearrange("p (e2 kq t) -> p e2 kq t",
                                                e2=4, kq=4)
                                [:, ei, kq, t4 * 512:(t4 + 1) * 512],
                                start=st, stop=last),
                              waits=[DG, VE],
                              inc=(PE if lastall else None))
                    if eg == 0:
                        T(lambda e, hc=hc, t4=t4: e.matmul(
                            ps[4 + t4][:, :], dnb_sb[0:16, hc * 128:(hc + 1) * 128],
                            scT[0:16, t4 * 512:(t4 + 1) * 512],
                            start=True, stop=True), waits=[DS],
                          inc=(PE if t4 == 3 else None))
                for t4 in range(4):
                    V(lambda e, t4=t4: e.tensor_copy(pool[1][:], ps[t4][:, :]),
                      waits=[PE])
                    if eg == 0:
                        V(lambda e, t4=t4: e.tensor_add(pool[1][:], pool[1][:],
                                                        ps[4 + t4][:, :]))
                    def accdma(e, eg=eg, th=th, hc=hc, t4=t4):
                        dst = hb["acc"].bitcast(F32)[:, hc * 4096 + th * 2048 +
                                                     t4 * 512:
                                                     hc * 4096 + th * 2048 +
                                                     t4 * 512 + 512]
                        if eg == 0:
                            return e.dma_start(out=dst, in_=pool[1][:])
                        return e.dma_start(out=dst, in_=pool[1][:],
                                           accum_op=ALU.add)
                    Dma("gpsimd", accdma, waits=[VE])


def emit_output(ctx):
    nc, P, kc, pr, ps, pool = ctx["nc"], ctx["P"], ctx["kc"], ctx["pr"], ctx["ps"], ctx["pool"]
    A1, A2 = ctx["A1"], ctx["A2"]
    hb = ctx["hbm"]
    DQ, DS, DG = ctx["sems"]["DQ"], ctx["sems"]["DS"], ctx["sems"]["DG"]
    PE, VE, SE = ctx["sems"]["PE"], ctx["sems"]["VE"], ctx["sems"]["SE"]
    out_t = ctx["out_t"]
    do_ffn = ctx["do_ffn"]

    def V(fn, waits=()):
        P.op("vector", fn, waits=waits, inc=VE)
    def Dma(q, fn, waits=()):
        ctr = {"sync": DQ, "scalar": DS, "gpsimd": DG}[q]
        P.op(q, fn, waits=waits, inc=ctr, k=16)

    Dma("sync", lambda e: e.dma_start(out=A1[:], in_=hb["x"]),
        waits=[VE, PE, SE, DG, DS])
    if do_ffn:
        for cb in range(8):
            slot = (cb % 2) * 8192
            Dma("scalar", lambda e, cb=cb, slot=slot: e.dma_start(
                out=A2[:, slot:slot + 8192].bitcast(F32),
                in_=hb["acc"][:, cb * 4096:(cb + 1) * 4096]),
                waits=[VE, DG])
            V(lambda e, cb=cb, slot=slot: e.tensor_add(
                A1[:].rearrange("p (tt cb2 i) -> p tt cb2 i", tt=32, cb2=8)
                [:, :, cb, :],
                A1[:].rearrange("p (tt cb2 i) -> p tt cb2 i", tt=32, cb2=8)
                [:, :, cb, :],
                A2[:, slot:slot + 8192].bitcast(F32)
                .rearrange("p (tt i) -> p tt i", tt=32)), waits=[DS, DQ])
    for tt in range(32):
        Dma("sync", lambda e, tt=tt: e.dma_start_transpose(
            out=A2[:, tt * 1024:(tt + 1) * 1024].rearrange("p (cb c) -> p cb c", cb=8),
            in_=A1[:, tt * 1024:(tt + 1) * 1024].rearrange("p (cb i) -> p cb i", cb=8)),
            waits=[VE])
    for part in range(4):
        Dma("gpsimd", lambda e, p_=part: e.dma_start(
            out=out_t.rearrange("(tt tp) c -> tp tt c", tp=128)
            [:, p_ * 8:(p_ + 1) * 8, :],
            in_=A2[:, p_ * 8192:(p_ + 1) * 8192].rearrange("p (tt c) -> p tt c", tt=8)),
            waits=[DQ])


def kernel(**inputs):
    from concourse.bass_utils import run_bass_kernel_spmd

    hp = host_prepare(inputs)
    x_full = np.asarray(inputs["L0_attn_x"], np.float32)
    nc, _ = build_program(dbg=(), n_inner=NL, do_ffn=True)

    shared = {
        "log_dt_s": hp["log_dt_s"], "A_re_s": hp["A_re_s"],
        "A_im_s": hp["A_im_s"], "C_re_s": hp["C_re_s"], "C_im_s": hp["C_im_s"],
        "lnwb": hp["lnwb"], "dskg": hp["dskg"],
        "w_out_t": hp["w_out_t"], "b_out": hp["b_out"],
        "router": np.ascontiguousarray(inputs["L0_ffn_router_weight"]),
        "gu": hp["gu"], "gub": hp["gub"],
        "down": np.ascontiguousarray(inputs["L0_ffn_down_proj"]),
        "downb": np.ascontiguousarray(inputs["L0_ffn_down_proj_bias"]),
    }
    for name, arr in _CONSTS.items():
        shared["c_" + name] = np.ascontiguousarray(arr)

    in_maps = []
    for core in range(8):
        m = dict(shared)
        m["x"] = np.ascontiguousarray(x_full[core])
        in_maps.append(m)

    res = run_bass_kernel_spmd(nc, in_maps, core_ids=list(range(8)))
    out = np.stack([np.asarray(res.results[i]["out"], np.float32)
                    for i in range(8)], axis=0)
    return out


# version 02
# speedup vs baseline: 1.4411x; 1.4411x over previous
import numpy as np
import ml_dtypes
import concourse.bass as bass
import concourse.mybir as mybir
from concourse.bass_utils import run_bass_kernel_spmd
from contextlib import ExitStack

F32 = mybir.dt.float32
BF16 = mybir.dt.bfloat16
I32 = mybir.dt.int32
AF = mybir.ActivationFunctionType
ALU = mybir.AluOpType
BF = ml_dtypes.bfloat16

L, H, NL, NST, Q, NCH = 4096, 1024, 4, 32, 128, 32
E, INTER = 16, 512
ALPHA, LIMIT = 1.702, 7.0
PI = np.pi
INV2PI = 1.0 / (2.0 * PI)
SQ2PI = 0.7978845608028654
GC3 = 0.044715

def host_constants():
    c = {}
    j = np.arange(Q)
    f = np.arange(Q + 1)
    ang = PI * np.outer(j, f) / Q
    Fre, Fim = np.cos(ang), -np.sin(ang)
    fwd_re = Fre[:, 0:128].copy()
    fwd_im = Fim[:, 0:128].copy()
    fwd_im[:, 0] = Fre[:, 128]
    c["fwd_re"] = fwd_re.astype(BF)
    c["fwd_im"] = fwd_im.astype(BF)
    c["fwdR_re"] = fwd_re[::-1, :].copy().astype(BF)
    fwdR_im = Fim[::-1, 0:128].copy()
    fwdR_im[:, 0] = Fre[::-1, 128]
    c["fwdR_im"] = fwdR_im.astype(BF)
    i_ = np.arange(Q)
    angi = PI * np.outer(f, i_) / Q
    w2 = np.full(Q + 1, 2.0); w2[0] = 1.0; w2[Q] = 1.0
    Gc = (w2[:, None] * np.cos(angi)) / (2 * Q)
    Gs = (-w2[:, None] * np.sin(angi)) / (2 * Q)
    inv_re = Gc[0:128, ::-1].copy()
    inv_im = Gs[0:128, ::-1].copy()
    inv_im[0, :] = Gc[128, ::-1]
    c["inv_re"] = inv_re.astype(BF)
    c["inv_im"] = inv_im.astype(BF)
    m = (Q - 1 - j).astype(np.float64)
    c["mvec_neg"] = (-m).astype(BF)[None, :]
    c["mvec_pos"] = m.astype(BF)[None, :]
    c["row_quart"] = np.full((1, 128), 0.25, BF)
    c["row_one"] = np.ones((1, 128), BF)
    c["ident"] = np.eye(128, dtype=np.float32)
    sel = np.zeros((128, 32, 32), np.float32)
    for v in range(32):
        sel[:, v, v] = 1.0
    c["lnsel"] = sel.reshape(128, 1024).astype(BF)
    p = np.arange(128)
    par = (p % 2).astype(np.float32)
    c["sgn_scan"] = (1.0 - 2.0 * par)[:, None].astype(np.float32)
    c["nsgn_scan"] = (2.0 * par - 1.0)[:, None].astype(np.float32)
    c["quart_re"] = ((1.0 - par) * 0.25)[:, None].astype(np.float32)
    c["mask_re"] = (1.0 - par)[:, None].astype(np.float32)
    c["mask_im"] = par[:, None].astype(np.float32)
    c["row_one_f"] = np.ones((1, 128), np.float32)
    c["row_quart_f"] = np.full((1, 128), 0.25, np.float32)
    c["e0vec"] = np.concatenate([np.ones((1, 1)), np.zeros((1, 127))],
                                axis=1).astype(BF)
    c["identb"] = np.eye(128).astype(BF)
    return c

_CONSTS = host_constants()

def host_prepare(inputs):
    d = {}
    Wo = np.asarray(inputs["L0_attn_W_out"])
    gorder = np.zeros(H, np.int64)
    for g in range(8):
        gorder[g * 128:g * 128 + 64] = np.arange(64 * g, 64 * g + 64)
        gorder[g * 128 + 64:g * 128 + 128] = 512 + np.arange(64 * g, 64 * g + 64)
    d["w_out_t"] = np.ascontiguousarray(np.transpose(Wo, (0, 2, 1))[:, gorder, :])
    gu = np.asarray(inputs["L0_ffn_gate_up_proj"])
    d["gu"] = np.ascontiguousarray(
        np.concatenate([gu[:, :, 0::2], gu[:, :, 1::2]], axis=2))
    gub = np.asarray(inputs["L0_ffn_gate_up_proj_bias"])
    gubd = np.concatenate([gub[:, 0::2], gub[:, 1::2]], axis=1)
    d["gub"] = np.ascontiguousarray(gubd.reshape(E, 8, 128).transpose(0, 2, 1))
    bo = np.asarray(inputs["L0_attn_b_out"])
    d["b_out"] = np.ascontiguousarray(bo.reshape(NL, 16, 128).transpose(0, 2, 1))
    ch2 = (np.arange(128) // 64)
    nidx = (np.arange(128) % 64) // 2
    cd = np.arange(4)
    pm = np.arange(128)
    chmap = (ch2[:, None, None] * 512 + 4 * pm[None, None, :] + cd[None, :, None])
    def scanp(a):
        a = np.asarray(a)
        if a.ndim == 2:
            r = a[:, chmap]
        else:
            r = a[:, chmap, nidx[:, None, None]]
        return np.ascontiguousarray(r.reshape(NL, 128, 512))
    d["log_dt_s"] = scanp(inputs["L0_attn_log_dt"])
    d["A_re_s"] = scanp(inputs["L0_attn_A_re_log"])
    d["A_im_s"] = scanp(inputs["L0_attn_A_im"])
    d["C_re_s"] = scanp(inputs["L0_attn_C_re"])
    d["C_im_s"] = scanp(inputs["L0_attn_C_im"])
    lw = np.asarray(inputs["L0_attn_ln_w"]).reshape(NL, 8, 128)
    lb = np.asarray(inputs["L0_attn_ln_b"]).reshape(NL, 8, 128)
    wb = np.stack([lw, lb], axis=-1)
    d["lnwb"] = np.ascontiguousarray(wb.transpose(0, 2, 1, 3).reshape(NL, 128, 16))
    ds = np.asarray(inputs["L0_attn_D_skip"])
    dsg = np.zeros((NL, H), np.float32)
    for g in range(8):
        for tl in range(64):
            qq = 4 * (16 * g + tl // 4) + tl % 4
            dsg[:, g * 128 + tl] = ds[:, qq]
            dsg[:, g * 128 + 64 + tl] = ds[:, qq + 512]
    d["dskg"] = np.ascontiguousarray(dsg)
    d["gorder"] = gorder
    return d

class Ctr:
    def __init__(self, h, name):
        self.h, self.name, self.n = h, name, 0

import types as _types

def _freeze(fn):
    clo = getattr(fn, "__closure__", None)
    if not clo:
        return fn
    cells = tuple(_types.CellType(c.cell_contents) for c in clo)
    g = _types.FunctionType(fn.__code__, fn.__globals__, fn.__name__,
                            fn.__defaults__, cells)
    g.__kwdefaults__ = fn.__kwdefaults__
    return g


class Prog:
    ENG = ["sync", "vector", "scalar", "tensor", "gpsimd"]
    DMAQ = ("sync", "scalar")
    def __init__(self, all_ctrs=None, serialize=False, dma_ctrs=()):
        self.ops = {e: [] for e in self.ENG}
        self.waited = {e: {} for e in self.ENG}
        self.all_ctrs = all_ctrs or []
        self.dma_ctrs = list(dma_ctrs)
        self.serialize = serialize
    ENG_CTR = {}
    def op(self, eng, fn, waits=(), inc=None, k=1):
        fn = _freeze(fn)
        if self.serialize:
            is_dma_q = eng in ("sync", "scalar", "gpsimd") and inc is not None                 and k == 16
            waits = list(waits) + list(self.all_ctrs)
            for oe in self.ENG:
                if oe == eng:
                    continue
                lst = self.ops[oe]
                if lst and lst[-1][2] is None:
                    ctr = self.ENG_CTR.get(oe)
                    if ctr is not None:
                        ctr.n += 1
                        lst[-1][2] = ctr
                        lst[-1][3] = 1
        wl = []
        for w in waits:
            ctr, val = (w, w.n) if isinstance(w, Ctr) else (w[0], w[1])
            if self.waited[eng].get(ctr.name, -1) < val:
                self.waited[eng][ctr.name] = val
                wl.append((ctr, val))
        if inc is not None:
            inc.n += k
        self.ops[eng].append([wl, fn, inc, k])
    def finish(self):
        for oe in self.ENG:
            lst = self.ops[oe]
            if lst and lst[-1][2] is None:
                ctr = self.ENG_CTR.get(oe)
                if ctr is not None:
                    ctr.n += 1
                    lst[-1][2] = ctr
                    lst[-1][3] = 1
    def emit(self, block):
        self.finish()
        for eng in self.ENG:
            ops = self.ops[eng]
            if not ops:
                continue
            def body(e, ops=ops):
                for wl, fn, inc, k in ops:
                    for ctr, val in wl:
                        e.wait_ge(ctr.h, val)
                    inst = fn(e)
                    if inc is not None:
                        if inst is None:
                            inst = (e.engine_nop() if hasattr(e, "engine_nop")
                                    else e.nop())
                        inst.then_inc(inc.h, k)
            getattr(block, eng)(body)

def build_program(dbg=(), n_inner=NL, do_ffn=True):
    dbg = set(dbg)
    nc = bass.Bass("TRN2", target_bir_lowering=False, debug=False)

    x_in = nc.dram_tensor("x", [L, H], F32, kind="ExternalInput").ap()
    out_t = nc.dram_tensor("out", [L, H], F32, kind="ExternalOutput").ap()
    pr = {}
    def inp(name, shape, dt=F32):
        pr[name] = nc.dram_tensor(name, shape, dt, kind="ExternalInput").ap()
    inp("log_dt_s", [NL, 128, 512]); inp("A_re_s", [NL, 128, 512])
    inp("A_im_s", [NL, 128, 512]); inp("C_re_s", [NL, 128, 512])
    inp("C_im_s", [NL, 128, 512])
    inp("lnwb", [NL, 128, 16]); inp("dskg", [NL, H])
    inp("w_out_t", [NL, H, 2 * H]); inp("b_out", [NL, 128, 16])
    inp("router", [H, E]); inp("gu", [E, H, 2 * INTER]); inp("gub", [E, 128, 8])
    inp("down", [E, INTER, H]); inp("downb", [E, H])
    cst = {}
    for name, arr in _CONSTS.items():
        dt = BF16 if arr.dtype == BF else F32
        cst[name] = nc.dram_tensor("c_" + name, list(arr.shape), dt,
                                   kind="ExternalInput").ap()
    hbm_h = nc.dram_tensor("s_h", [128, 8 * L], BF16, kind="Internal").ap()
    hbm_x = nc.dram_tensor("s_x", [128, 8 * L], BF16, kind="Internal").ap()
    hbm_z = nc.dram_tensor("s_z", [128, 8 * L], BF16, kind="Internal").ap()
    hbm_y = nc.dram_tensor("s_y", [128, 8 * L], BF16, kind="Internal").ap()
    hbm_ap = nc.dram_tensor("s_ap", [128, 8 * 8192], BF16, kind="Internal").ap()
    hbm_acc = nc.dram_tensor("s_acc", [128, 8 * L], F32, kind="Internal").ap()

    dbg_outs = {}
    def dbgt(name, shape, dt=F32):
        if name in dbg:
            dbg_outs[name] = nc.dram_tensor("dbg_" + name, shape, dt,
                                            kind="ExternalOutput").ap()
            return dbg_outs[name]
        return None

    es = ExitStack()
    with es:
        def S(name, pdim, fdim, dt):
            return es.enter_context(nc.sbuf_tensor(name, [pdim, fdim], dt))

        kc = {}
        for name, arr in _CONSTS.items():
            dt = BF16 if arr.dtype == BF else F32
            kc[name] = S("k_" + name, arr.shape[0], arr.shape[1], dt)
        kbias = S("kbias", 128, 2, F32)

        A1 = S("A1", 128, 32768, BF16)
        A2 = S("A2", 128, 32768, BF16)
        pool = [S("pool%d" % i, 128, 512, F32) for i in range(6)]
        s_state = S("s_state", 128, 512, F32)
        s_sw = S("s_sw", 128, 512, F32)
        s_ta = S("s_ta", 128, 512, F32)
        s_tb = S("s_tb", 128, 512, F32)
        Lt1 = S("Lt1", 128, 512, F32)
        Lt2 = S("Lt2", 128, 512, F32)
        wkvec = S("wkvec", 128, 1024, BF16)
        rlayb = S("rlayb", 128, 2048, BF16)
        rlayf = S("rlayf", 128, 1024, F32)
        rstgb = S("rstgb", 1, 2048, BF16)
        rstgf = S("rstgf", 1, 1024, F32)
        stg_st = S("stg_st", 1, 2048, F32)
        kk = S("kk", 128, 128, BF16)
        khAre = S("khAre", 128, 128, BF16)
        khAim = S("khAim", 128, 128, BF16)
        khBre = S("khBre", 128, 128, BF16)
        khBim = S("khBim", 128, 128, BF16)
        dskrow = S("dskrow", 1, H, BF16)
        st_sum = S("st_sum", 32, 128, F32)
        st_rstd = S("st_rstd", 32, 128, F32)
        st_ns2 = S("st_ns2", 32, 128, F32)
        sq_t = S("sq_t", 128, 1024, BF16)
        lnwb_sb = S("lnwb_sb", 128, 16, F32)
        bo_sb = S("bo_sb", 128, 16, F32)
        bo2_sb = S("bo2_sb", 128, 16, F32)
        gub_sb = S("gub_sb", 128, 8, F32)
        gub2_sb = S("gub2_sb", 128, 8, F32)
        stg_sc = S("stg_sc", 1, 2048, BF16)
        dnb_sb = S("dnb_sb", 16, H, BF16)
        scT = S("scT", 16, 2048, BF16)
        rtr_sb = S("rtr_sb", 128, 128, BF16)
        ps = [es.enter_context(nc.psum_tensor('psb%d' % i, [128, 512], F32)) for i in range(8)]

        DQ = Ctr(es.enter_context(nc.semaphore("dq")), "dq")
        DS = Ctr(es.enter_context(nc.semaphore("ds")), "ds")
        DG = Ctr(es.enter_context(nc.semaphore("dg")), "dg")
        PE = Ctr(es.enter_context(nc.semaphore("pe")), "pe")
        VE = Ctr(es.enter_context(nc.semaphore("ve")), "ve")
        SE = Ctr(es.enter_context(nc.semaphore("se")), "se")
        GP = Ctr(es.enter_context(nc.semaphore("gp")), "gp")
        P = Prog(all_ctrs=[DQ, DS, DG, PE, VE, SE, GP], serialize=True,
                 dma_ctrs=[DQ, DS, DG])
        P.ENG_CTR = {"tensor": PE, "vector": VE, "scalar": SE}

        def bfview(t, c0, c1):
            return t[:, c0:c1]
        tabB = lambda: A1[:, 0:8192]
        tabA1 = lambda: A1[:, 8192:16384]
        zblk0 = lambda: A1[:, 16384:20480]
        zblk1 = lambda: A1[:, 20480:24576]
        zg = lambda: A1[:, 24576:28672]
        ztm = lambda: A1[:, 28672:32768]
        sp_re = lambda: A1[:, 0:4096]
        sp_im = lambda: A1[:, 4096:8192]
        spT = lambda: A1[:, 8192:12288]
        ytm = lambda: A1[:, 12288:16384]
        tabA2 = lambda: A1[:, 16384:24576]
        PUr = lambda: A2[:, 0:16384]
        wres = lambda: A2[:, 0:16384]
        yhalf = lambda: A2[:, 16384:32768]
        xch = lambda: A1[:]

        def frac_chain(dst_f32, src_psum_ap, tmp_i32_ap):
            P.op("vector", lambda e: e.tensor_copy(tmp_i32_ap, src_psum_ap))
            P.op("vector", lambda e: e.tensor_sub(dst_f32, src_psum_ap, tmp_i32_ap),
                 inc=VE)

        for name in _CONSTS:
            P.op("sync", lambda e, n=name: e.dma_start(out=kc[n][:], in_=cst[n]),
                 inc=DQ, k=16)
        P.op("vector", lambda e: e.memset(kbias[:], 0.0))
        P.op("vector", lambda e: e.memset(kbias[:, 0:1], 1e-5))
        P.op("vector", lambda e: e.memset(kbias[:, 1:2], float(np.pi / 2)))
        P.op("vector", lambda e: e.memset(s_state[:], 0.0), inc=VE)
        for part in range(4):
            P.op("gpsimd", lambda e, p_=part: e.dma_start(
                out=A2[:, p_ * 8192:(p_ + 1) * 8192].rearrange("p (tt c) -> p tt c", tt=8),
                in_=x_in.rearrange("(tt tp) c -> tp tt c", tp=128)
                    [:, p_ * 8:(p_ + 1) * 8, :]), inc=DG, k=16)
        for tt in range(32):
            P.op("sync", lambda e, tt=tt: e.dma_start_transpose(
                out=xch()[:, tt * 1024:(tt + 1) * 1024].rearrange("p (cb i) -> p cb i", cb=8),
                in_=A2[:, tt * 1024:(tt + 1) * 1024].rearrange("p (cb c) -> p cb c", cb=8)),
                 waits=[DG], inc=DQ, k=16)

        def emit_ln_stats(src_fn):
            nmm = 0
            for tt in range(32):
                for cb in range(8):
                    last = (tt == 31 and cb == 7)
                    P.op("vector", lambda e, tt=tt, cb=cb: e.tensor_mul(
                        sq_t[:, 0:128], src_fn(tt, cb), src_fn(tt, cb)),
                         waits=[PE, DQ, DG], inc=VE)
                    P.op("tensor", lambda e, tt=tt, cb=cb, st=(nmm == 0): e.matmul(
                        ps[6][0:32, 0:128],
                        kc["lnsel"][:].rearrange("p (v c) -> p v c", v=32)[:, tt, :],
                        src_fn(tt, cb), start=st, stop=last), waits=[VE])
                    P.op("tensor", lambda e, tt=tt, cb=cb, st=(nmm == 0): e.matmul(
                        ps[7][0:32, 0:128],
                        kc["lnsel"][:].rearrange("p (v c) -> p v c", v=32)[:, tt, :],
                        sq_t[:, 0:128], start=st, stop=last), inc=PE)
                    nmm += 1
            P.op("vector", lambda e: e.tensor_scalar_mul(st_sum[0:32, :], ps[6][0:32, 0:128], 1.0 / H),
                 waits=[PE])
            P.op("vector", lambda e: e.tensor_mul(st_rstd[0:32, :], st_sum[0:32, :], st_sum[0:32, :]))
            P.op("vector", lambda e: e.scalar_tensor_tensor(
                st_rstd[0:32, :], ps[7][0:32, 0:128], 1.0 / H, st_rstd[0:32, :],
                ALU.mult, ALU.subtract), inc=VE)
            P.op("scalar", lambda e: e.activation(st_rstd[0:32, :], st_rstd[0:32, :],
                                                  AF.Ln, bias=kbias[0:32, 0:1]), waits=[VE])
            P.op("scalar", lambda e: e.activation(st_rstd[0:32, :], st_rstd[0:32, :],
                                                  AF.Exp, scale=-0.5), inc=SE)
            P.op("vector", lambda e: e.scalar_tensor_tensor(
                st_ns2[0:32, :], st_sum[0:32, :], -1.0, st_rstd[0:32, :],
                ALU.mult, ALU.mult), waits=[SE], inc=VE)


        P.op("vector", lambda e: e.memset(pool[5][:], 1.0), inc=VE)

        def emit_ln_bcast32(tt):
            if tt % 8 == 0:
                P.op("sync", lambda e, tt=tt: e.dma_start(
                    out=stg_st[0:1, 0:1024], in_=st_rstd[tt:tt + 8, :]),
                     waits=[VE, PE], inc=DQ, k=16)
                P.op("sync", lambda e, tt=tt: e.dma_start(
                    out=stg_st[0:1, 1024:2048], in_=st_ns2[tt:tt + 8, :]),
                     inc=DQ, k=16)
            o = (tt % 8) * 128
            P.op("tensor", lambda e, o=o: e.matmul(
                ps[4][:, 0:128], pool[5][0:1, 0:128],
                stg_st[0:1, o:o + 128],
                start=True, stop=True), waits=[DQ, VE])
            P.op("tensor", lambda e, o=o: e.matmul(
                ps[5][:, 0:128], pool[5][0:1, 0:128],
                stg_st[0:1, 1024 + o:1024 + o + 128],
                start=True, stop=True), inc=PE)

        def apply_ln_tile(tt, src_ap, dst_ap, use_wb):
            def bc(psb):
                return psb[:, 0:128].unsqueeze(1).broadcast_to([128, 8, 128])
            P.op("vector", lambda e: e.tensor_mul(sq_t[:, 0:1024]
                                                  .rearrange("p (cb i) -> p cb i", cb=8),
                                                  src_ap, bc(ps[4])), waits=[PE])
            P.op("vector", lambda e: e.tensor_add(
                sq_t[:, 0:1024].rearrange("p (cb i) -> p cb i", cb=8),
                sq_t[:, 0:1024].rearrange("p (cb i) -> p cb i", cb=8), bc(ps[5])))
            if use_wb:
                P.op("vector", lambda e: e.tensor_mul(
                    sq_t[:, 0:1024].rearrange("p (cb i) -> p cb i", cb=8),
                    sq_t[:, 0:1024].rearrange("p (cb i) -> p cb i", cb=8),
                    lnwb_sb[:, 0:16].rearrange("p (cb two) -> p cb two", cb=8)
                    [:, :, 0:1].broadcast_to([128, 8, 128])))
                P.op("vector", lambda e: e.tensor_add(
                    dst_ap,
                    sq_t[:, 0:1024].rearrange("p (cb i) -> p cb i", cb=8),
                    lnwb_sb[:, 0:16].rearrange("p (cb two) -> p cb two", cb=8)
                    [:, :, 1:2].broadcast_to([128, 8, 128])), inc=VE)
            else:
                P.op("vector", lambda e: e.tensor_copy(
                    dst_ap, sq_t[:, 0:1024].rearrange("p (cb i) -> p cb i", cb=8)),
                     inc=VE)

        emit_ln_stats(lambda tt, cb: xch()[:, tt * 1024 + cb * 128: tt * 1024 + cb * 128 + 128])
        for tt in range(32):
            emit_ln_bcast32(tt)
            apply_ln_tile(tt,
                          xch()[:, tt * 1024:(tt + 1) * 1024].rearrange("p (cb i) -> p cb i", cb=8),
                          A2[:, tt * 1024:(tt + 1) * 1024].rearrange("p (cb i) -> p cb i", cb=8),
                          use_wb=False)
        for hq in range(4):
            P.op("sync", lambda e, hq=hq: e.dma_start(
                out=hbm_h[:, hq * 8192:(hq + 1) * 8192],
                in_=A2[:, hq * 8192:(hq + 1) * 8192]), waits=[VE], inc=DQ, k=16)
            P.op("scalar", lambda e, hq=hq: e.dma_start(
                out=hbm_x[:, hq * 8192:(hq + 1) * 8192],
                in_=A1[:, hq * 8192:(hq + 1) * 8192]), waits=[VE], inc=DS, k=16)
        d = dbgt("z1", [128, 8 * L], BF16)
        if d is not None:
            for hq in range(4):
                P.op("sync", lambda e, hq=hq, d=d: e.dma_start(
                    out=d.rearrange("p (a b) -> p a b", a=4)[:, hq, :],
                    in_=A2[:, hq * 8192:(hq + 1) * 8192]), inc=DQ, k=16)

        ctx = dict(nc=nc, P=P, es=es, S=S, kc=kc, pr=pr, ps=ps, pool=pool,
                   A1=A1, A2=A2, sq_t=sq_t,
                   sems=dict(DQ=DQ, DS=DS, DG=DG, PE=PE, VE=VE, SE=SE, GP=GP),
                   hbm=dict(h=hbm_h, x=hbm_x, z=hbm_z, y=hbm_y, ap=hbm_ap, acc=hbm_acc),
                   regions=dict(tabB=tabB, tabA1=tabA1, zblk0=zblk0, zblk1=zblk1,
                                zg=zg, ztm=ztm, sp_re=sp_re, sp_im=sp_im, spT=spT,
                                ytm=ytm, tabA2=tabA2, PUr=PUr, wres=wres,
                                yhalf=yhalf, xch=xch),
                   smalls=dict(s_state=s_state, s_sw=s_sw, s_ta=s_ta, s_tb=s_tb,
                               Lt1=Lt1, Lt2=Lt2, wkvec=wkvec, rlayb=rlayb,
                               rlayf=rlayf, rstgb=rstgb, rstgf=rstgf,
                               stg_st=stg_st, kk=kk, khAre=khAre,
                               khAim=khAim, khBre=khBre, khBim=khBim,
                               dskrow=dskrow, st_sum=st_sum, st_rstd=st_rstd,
                               st_ns2=st_ns2, lnwb_sb=lnwb_sb, bo_sb=bo_sb,
                               bo2_sb=bo2_sb, gub_sb=gub_sb, gub2_sb=gub2_sb,
                               stg_sc=stg_sc, scT=scT,
                               dnb_sb=dnb_sb, rtr_sb=rtr_sb,
                               kbias=kbias),
                   helpers=dict(emit_ln_stats=emit_ln_stats,
                                emit_ln_bcast32=emit_ln_bcast32,
                                apply_ln_tile=apply_ln_tile,
                                frac_chain=frac_chain),
                   dbgt=dbgt, x_in=x_in, out_t=out_t, do_ffn=do_ffn)

        emit_layers(ctx, n_inner, do_ffn)

        P.op("gpsimd", lambda e: e.engine_nop(), waits=[DQ, DS, DG, PE, VE, SE, GP])
        block = es.enter_context(nc.Block())
        P.emit(block)
    return nc, dbg_outs


def emit_layers(ctx, n_inner, do_ffn):
    nc, P, kc, pr, ps, pool = ctx["nc"], ctx["P"], ctx["kc"], ctx["pr"], ctx["ps"], ctx["pool"]
    A1, A2, sq_t = ctx["A1"], ctx["A2"], ctx["sq_t"]
    sm = ctx["smalls"]; rg = ctx["regions"]; hb = ctx["hbm"]
    DQ, DS, DG = ctx["sems"]["DQ"], ctx["sems"]["DS"], ctx["sems"]["DG"]
    PE, VE, SE, GP = ctx["sems"]["PE"], ctx["sems"]["VE"], ctx["sems"]["SE"], ctx["sems"]["GP"]
    dbgt = ctx["dbgt"]
    out_t = ctx["out_t"]
    Lt1, Lt2, wkvec = sm["Lt1"], sm["Lt2"], sm["wkvec"]
    rlayb, rlayf = sm["rlayb"], sm["rlayf"]
    rstgb, rstgf = sm["rstgb"], sm["rstgf"]
    kk = sm["kk"]; khAre, khAim = sm["khAre"], sm["khAim"]
    khBre, khBim = sm["khBre"], sm["khBim"]
    dskrow = sm["dskrow"]; lnwb_sb = sm["lnwb_sb"]
    bo_sb, bo2_sb = sm["bo_sb"], sm["bo2_sb"]
    gub_sb, gub2_sb = sm["gub_sb"], sm["gub2_sb"]
    stg_sc, dnb_sb, rtr_sb = sm["stg_sc"], sm["dnb_sb"], sm["rtr_sb"]
    scT = sm["scT"]
    s_state, s_sw, s_ta, s_tb = sm["s_state"], sm["s_sw"], sm["s_ta"], sm["s_tb"]
    stg_st = sm["stg_st"]
    st_sum, st_rstd, st_ns2 = sm["st_sum"], sm["st_rstd"], sm["st_ns2"]
    kbias = sm["kbias"]

    def V(fn, waits=()):
        P.op("vector", fn, waits=waits, inc=VE)
    def Sc(fn, waits=()):
        P.op("scalar", fn, waits=waits, inc=SE)
    def T(fn, waits=(), inc=PE):
        P.op("tensor", fn, waits=waits, inc=inc)
    def Dma(q, fn, waits=()):
        ctr = {"sync": DQ, "scalar": DS, "gpsimd": DG}[q]
        P.op(q, fn, waits=waits, inc=ctr, k=16)

    sqf = sq_t[:, 0:1024].bitcast(F32)

    def layer_smalls(l):
        for i, nm in enumerate(["log_dt_s", "A_re_s", "A_im_s", "C_re_s", "C_im_s"]):
            Dma("sync", lambda e, nm=nm, i=i: e.dma_start(out=pool[i][:], in_=pr[nm][l]),
                waits=[VE, PE, SE, DS, DG])
        Dma("scalar", lambda e: e.dma_start(out=lnwb_sb[:], in_=pr["lnwb"][l]),
            waits=[VE, PE])
        Dma("scalar", lambda e: e.dma_start(out=bo_sb[:], in_=pr["b_out"][l]))
        Dma("gpsimd", lambda e: e.dma_start(out=dskrow[0:1, :],
                                            in_=pr["dskg"][l].unsqueeze(0)),
            waits=[VE, PE])
        V(lambda e: e.tensor_scalar_mul(bo2_sb[:], bo_sb[:], 0.5), waits=[DS])
        X0, X1, P2, P3, P4 = pool[0][:], pool[1][:], pool[2][:], pool[3][:], pool[4][:]
        T1, T2, T3, T4 = pool[5][:], s_ta[:], s_tb[:], sqf
        T5, T6 = Lt1[:], Lt2[:]
        al, be = s_state[:], s_sw[:]
        Sc(lambda e: e.activation(T2, X0, AF.Exp), waits=[DQ])
        Sc(lambda e: e.activation(T3, X1, AF.Exp))
        V(lambda e: e.tensor_mul(al, T2, T3), waits=[SE])
        V(lambda e: e.scalar_tensor_tensor(be, T2, 1.0 / (2 * PI), P2,
                                           ALU.mult, ALU.mult))
        V(lambda e: e.tensor_copy(Lt1[:].bitcast(I32), be))
        V(lambda e: e.tensor_sub(be, be, Lt1[:].bitcast(I32)))
        Sc(lambda e: e.activation(T2, al, AF.Exp, scale=-1.0), waits=[VE])
        Sc(lambda e: e.activation(X0, be, AF.Sin, scale=-2 * PI, bias=kbias[:, 1:2]))
        V(lambda e: e.tensor_mul(T1, T2, X0), waits=[SE])
        Sc(lambda e: e.activation(X0, be, AF.Sin, scale=2 * PI), waits=[VE])
        V(lambda e: e.tensor_mul(X1, T2, X0), waits=[SE])
        V(lambda e: e.tensor_mul(X0, T3, T3))
        V(lambda e: e.tensor_mul(T2, P2, P2))
        V(lambda e: e.tensor_add(X0, X0, T2))
        V(lambda e: e.reciprocal(T2, X0))
        V(lambda e: e.tensor_scalar_add(X0, T1, -1.0))
        V(lambda e: e.tensor_mul(T4, X0, T3))
        V(lambda e: e.tensor_mul(T5, X1, T3))
        V(lambda e: e.tensor_mul(T3, X1, P2))
        V(lambda e: e.tensor_sub(T4, T3, T4))
        V(lambda e: e.tensor_mul(T4, T4, T2))
        V(lambda e: e.tensor_mul(T3, X0, P2))
        V(lambda e: e.tensor_add(T5, T5, T3))
        V(lambda e: e.scalar_tensor_tensor(T5, T5, -1.0, T2, ALU.mult, ALU.mult))
        V(lambda e: e.tensor_mul(T3, P4, T5))
        V(lambda e: e.tensor_mul(X0, P3, T4))
        V(lambda e: e.tensor_sub(X0, X0, T3))
        V(lambda e: e.tensor_mul(T3, P3, T5))
        V(lambda e: e.tensor_mul(T6, P4, T4))
        V(lambda e: e.tensor_add(T6, T6, T3))
        V(lambda e: e.tensor_mul(T3, T6, X1))
        V(lambda e: e.tensor_mul(T4, X0, T1))
        V(lambda e: e.tensor_sub(T4, T4, T3))
        V(lambda e: e.tensor_mul(T3, T6, T1))
        V(lambda e: e.tensor_mul(T5, X0, X1))
        V(lambda e: e.tensor_add(T5, T5, T3))
        V(lambda e: e.tensor_mul(T3, T4, T4))
        V(lambda e: e.tensor_mul(X0, T5, T5))
        V(lambda e: e.tensor_add(X0, X0, T3))
        Sc(lambda e: e.activation(X0, X0, AF.Ln), waits=[VE])
        V(lambda e: e.reciprocal(T3, T4), waits=[SE])
        V(lambda e: e.tensor_mul(T3, T5, T3))
        Sc(lambda e: e.activation(T3, T3, AF.Arctan), waits=[VE])
        V(lambda e: e.tensor_single_scalar(X1, T4, 0.0, ALU.is_lt), waits=[SE])
        V(lambda e: e.tensor_scalar_mul(T3, T3, 1.0 / (4 * PI)))
        V(lambda e: e.scalar_tensor_tensor(X1, X1, 0.25, T3, ALU.mult, ALU.add))
        Sc(lambda e: e.activation(T1, al, AF.Exp, scale=-128.0), waits=[VE])
        V(lambda e: e.tensor_scalar_mul(T3, be, 128.0), waits=[SE])
        i32a = Lt2[:].bitcast(I32)
        V(lambda e: e.tensor_copy(i32a, T3))
        V(lambda e: e.tensor_sub(T4, T3, i32a))
        Sc(lambda e: e.activation(T3, T4, AF.Abs), waits=[VE])
        Sc(lambda e: e.activation(T5, T3, AF.Sin, scale=-2 * PI, bias=kbias[:, 1:2]))
        Sc(lambda e: e.activation(T3, T4, AF.Sin, scale=2 * PI))
        V(lambda e: e.tensor_mul(T5, T1, T5), waits=[SE])
        V(lambda e: e.tensor_mul(T6, T1, T3))
        V(lambda e: e.tensor_scalar_mul(T6, T6, kc["nsgn_scan"][:, 0:1]))
        V(lambda e: e.scalar_tensor_tensor(P2, X0, 0.25, al, ALU.mult, ALU.add))
        V(lambda e: e.tensor_scalar_add(P2, P2, 0.34657359027997264))
        Sc(lambda e: e.activation(P3, P2, AF.Exp), waits=[VE])
        V(lambda e: e.tensor_sub(P2, X1, be))
        i32b = T4.bitcast(I32) if hasattr(T4, "bitcast") else None
        i32v = sq_t[:, 0:1024].bitcast(I32)[:, 0:512]
        V(lambda e: e.tensor_copy(i32v, P2))
        V(lambda e: e.tensor_sub(P4, P2, i32v))
        Sc(lambda e: e.activation(P2, P4, AF.Abs), waits=[VE])
        Sc(lambda e: e.activation(T1, P2, AF.Sin, scale=-2 * PI, bias=kbias[:, 1:2]))
        Sc(lambda e: e.activation(P2, P4, AF.Sin, scale=2 * PI))
        V(lambda e: e.tensor_mul(T1, P3, T1), waits=[SE])
        V(lambda e: e.tensor_mul(P2, P3, P2))
        V(lambda e: e.tensor_scalar_mul(T1, T1, kc["mask_re"][:, 0:1]))
        V(lambda e: e.scalar_tensor_tensor(T1, P2, kc["mask_im"][:, 0:1], T1,
                                           ALU.mult, ALU.add))
        V(lambda e: e.memset(wkvec[:], 0.0))
        V(lambda e: e.tensor_copy(
            wkvec[0:64, :].rearrange("p (pm cd two) -> p pm cd two", pm=128, cd=4)
            [:, :, :, 0],
            T1[0:64, 0:512].rearrange("p (cd pm) -> p pm cd", cd=4)))
        V(lambda e: e.tensor_copy(
            wkvec[64:128, :].rearrange("p (pm cd two) -> p pm cd two", pm=128, cd=4)
            [:, :, :, 1],
            T1[64:128, 0:512].rearrange("p (cd pm) -> p pm cd", cd=4)))
        V(lambda e: e.tensor_scalar_mul(P2, be, kc["sgn_scan"][:, 0:1]))
        V(lambda e: e.tensor_scalar(P3, X1, kc["sgn_scan"][:, 0:1],
                                    kc["quart_re"][:, 0:1], ALU.mult, ALU.add))
        def transpose_qty(srcv, writer):
            for t in range(4):
                T(lambda e, t=t, srcv=srcv: e.transpose(
                    ps[0][:, 0:128], srcv[:, t * 128:(t + 1) * 128], kc["ident"][:]),
                  waits=[VE, SE])
                writer(t)
        def wr_split(qoff):
            def inner(t):
                V(lambda e: e.tensor_copy(T4[:, 0:128], ps[0][:, 0:128]), waits=[PE])
                V(lambda e, t=t: e.tensor_copy(
                    rlayb[:, qoff * 512 + t * 128:qoff * 512 + t * 128 + 128],
                    T4[:, 0:128]))
                V(lambda e, t=t: e.tensor_sub(
                    T4[:, 128:256], T4[:, 0:128],
                    rlayb[:, qoff * 512 + t * 128:qoff * 512 + t * 128 + 128]))
                V(lambda e, t=t: e.tensor_copy(
                    rlayb[:, (qoff + 1) * 512 + t * 128:(qoff + 1) * 512 + t * 128 + 128],
                    T4[:, 128:256]))
            return inner
        def wr_f32(qoff):
            def inner(t):
                V(lambda e, t=t: e.tensor_copy(
                    rlayf[:, qoff * 512 + t * 128:qoff * 512 + t * 128 + 128],
                    ps[0][:, 0:128]), waits=[PE])
            return inner
        transpose_qty(al, wr_split(0))
        transpose_qty(P2, wr_split(2))
        transpose_qty(X0, wr_f32(0))
        transpose_qty(P3, wr_f32(1))
        for nm, tsr, shp, dt_ in [("rlayb", rlayb, [128, 2048], BF16),
                                  ("rlayf", rlayf, [128, 1024], F32),
                                  ("Lt1", Lt1, [128, 512], F32),
                                  ("Lt2", Lt2, [128, 512], F32),
                                  ("wkvec", wkvec, [128, 1024], BF16)]:
            dd = dbgt("sm_%s_l%d" % (nm, l), shp, dt_)
            if dd is not None:
                Dma("sync", lambda e, dd=dd, tsr=tsr: e.dma_start(out=dd, in_=tsr[:]),
                    waits=[VE, PE, SE])
        V(lambda e: e.memset(s_state[:], 0.0))

    def layer_ln(l):
        hslot = A2[:, 16384:24576]
        def hload(qq, waits=()):
            Dma("sync", lambda e, qq=qq: e.dma_start(
                out=hslot, in_=hb["h"][:, qq * 8192:(qq + 1) * 8192]),
                waits=[VE, PE] + list(waits))
        def src_fn(tt, cb):
            return hslot[:, (tt % 8) * 1024 + cb * 128:(tt % 8) * 1024 + cb * 128 + 128]
        nmm = [0]
        for qq in range(4):
            hload(qq)
            for tt8 in range(8):
                tt = qq * 8 + tt8
                for cb in range(8):
                    last = (tt == 31 and cb == 7)
                    st = (nmm[0] == 0)
                    V(lambda e, tt=tt, cb=cb: e.tensor_mul(
                        sq_t[:, 0:128], src_fn(tt, cb), src_fn(tt, cb)),
                      waits=[PE, DQ])
                    T(lambda e, tt=tt, cb=cb, st=st, last=last: e.matmul(
                        ps[6][0:32, 0:128],
                        kc["lnsel"][:].rearrange("p (v c) -> p v c", v=32)[:, tt, :],
                        src_fn(tt, cb), start=st, stop=last), waits=[VE, DQ], inc=None)
                    T(lambda e, tt=tt, cb=cb, st=st, last=last: e.matmul(
                        ps[7][0:32, 0:128],
                        kc["lnsel"][:].rearrange("p (v c) -> p v c", v=32)[:, tt, :],
                        sq_t[:, 0:128], start=st, stop=last))
                    nmm[0] += 1
        V(lambda e: e.tensor_scalar_mul(st_sum[0:32, :], ps[6][0:32, 0:128], 1.0 / H),
          waits=[PE])
        V(lambda e: e.tensor_mul(st_rstd[0:32, :], st_sum[0:32, :], st_sum[0:32, :]))
        V(lambda e: e.scalar_tensor_tensor(
            st_rstd[0:32, :], ps[7][0:32, 0:128], 1.0 / H, st_rstd[0:32, :],
            ALU.mult, ALU.subtract))
        Sc(lambda e: e.activation(st_rstd[0:32, :], st_rstd[0:32, :], AF.Ln,
                                  bias=kbias[0:32, 0:1]), waits=[VE])
        Sc(lambda e: e.activation(st_rstd[0:32, :], st_rstd[0:32, :], AF.Exp, scale=-0.5))
        V(lambda e: e.scalar_tensor_tensor(
            st_ns2[0:32, :], st_sum[0:32, :], -1.0, st_rstd[0:32, :],
            ALU.mult, ALU.mult), waits=[SE])
        for bp in range(4):
            for qq in range(4):
                hload(qq, waits=[DG])
                for tt8 in range(8):
                    tt = qq * 8 + tt8
                    if tt % 8 == 0:
                        Dma("scalar", lambda e, tt=tt: e.dma_start(
                            out=stg_st[0:1, 0:1024], in_=st_rstd[tt:tt + 8, :]),
                            waits=[VE, PE])
                        Dma("scalar", lambda e, tt=tt: e.dma_start(
                            out=stg_st[0:1, 1024:2048], in_=st_ns2[tt:tt + 8, :]))
                    o = (tt % 8) * 128
                    T(lambda e, o=o: e.matmul(
                        ps[4][:, 0:128], kc["row_one_f"][0:1, :],
                        stg_st[0:1, o:o + 128], start=True, stop=True),
                      waits=[DS, VE, DQ], inc=None)
                    T(lambda e, o=o: e.matmul(
                        ps[5][:, 0:128], kc["row_one_f"][0:1, :],
                        stg_st[0:1, 1024 + o:1024 + o + 128], start=True, stop=True))
                    for half, cb in ((0, bp), (1, bp + 4)):
                        zdst = (rg["zblk0"]() if half == 0 else rg["zblk1"]())
                        V(lambda e, tt=tt, cb=cb: e.tensor_mul(
                            sq_t[:, 0:128], src_fn(tt, cb), ps[4][:, 0:128]),
                          waits=[PE, DQ])
                        V(lambda e: e.tensor_add(sq_t[:, 0:128], sq_t[:, 0:128],
                                                 ps[5][:, 0:128]))
                        V(lambda e, cb=cb, zdst=zdst, tt=tt: e.tensor_scalar(
                            zdst[:, tt * 128:tt * 128 + 128], sq_t[:, 0:128],
                            lnwb_sb[:, 2 * cb:2 * cb + 1],
                            lnwb_sb[:, 2 * cb + 1:2 * cb + 2], ALU.mult, ALU.add))
            for half, cb in ((0, bp), (1, bp + 4)):
                zsrc = (rg["zblk0"]() if half == 0 else rg["zblk1"]())
                Dma("gpsimd", lambda e, cb=cb, zsrc=zsrc: e.dma_start(
                    out=hb["z"][:, cb * 4096:(cb + 1) * 4096], in_=zsrc),
                    waits=[VE])

    def zg_load(g):
        bl, half = g // 2, g % 2
        Dma("sync", lambda e, bl=bl, half=half: e.dma_start(
            out=rg["zg"]()[0:64, :],
            in_=hb["z"][half * 64:half * 64 + 64, bl * 4096:(bl + 1) * 4096]),
            waits=[VE, PE, DG])
        Dma("sync", lambda e, bl=bl, half=half: e.dma_start(
            out=rg["zg"]()[64:128, :],
            in_=hb["z"][half * 64:half * 64 + 64, (bl + 4) * 4096:(bl + 5) * 4096]))

    def ztm_build():
        Dma("sync", lambda e: e.dma_start_transpose(
            out=rg["ztm"]()[:].rearrange("p (c i) -> p c i", c=32),
            in_=rg["zg"]()[:].rearrange("p (c i) -> p c i", c=32)), waits=[DQ])

    def btable_build(g):
        tabB = rg["tabB"]()
        fb, i32b = pool[0][:], pool[1][:].bitcast(I32)
        snb = pool[2][:].bitcast(BF16)[:, 0:512]
        mgb = pool[2][:].bitcast(BF16)[:, 512:1024]
        def refill(pm):
            Dma("scalar", lambda e, pm=pm: e.dma_start(
                out=rstgb[0:1, :], in_=rlayb[pm:pm + 1, :]), waits=[PE, VE, SE])
            Dma("scalar", lambda e, pm=pm: e.dma_start(
                out=rstgf[0:1, :], in_=rlayf[pm:pm + 1, :]))
        for pmr in range(16):
            pm = 16 * g + pmr
            refill(pm)
            bo = 0
            fo = 0
            T(lambda e, bo=bo: e.matmul(ps[0][:, :], kc["mvec_pos"][0:1, :],
                                        rstgb[0:1, bo + 1024:bo + 1536],
                                        start=True, stop=False),
              waits=[DS, VE, SE], inc=None)
            T(lambda e, bo=bo: e.matmul(ps[0][:, :], kc["mvec_pos"][0:1, :],
                                        rstgb[0:1, bo + 1536:bo + 2048],
                                        start=False, stop=False), inc=None)
            T(lambda e, fo=fo: e.matmul(ps[0][:, :], kc["row_one_f"][0:1, :],
                                        rstgf[0:1, fo + 512:fo + 1024],
                                        start=False, stop=True))
            V(lambda e: e.tensor_copy(i32b, ps[0][:, :]), waits=[PE])
            V(lambda e: e.tensor_sub(fb, ps[0][:, :], i32b))
            Sc(lambda e, pmr=pmr: e.activation(
                tabB[:, pmr * 512:(pmr + 1) * 512], fb, AF.Sin, scale=2 * PI),
               waits=[VE])
        for pmr in range(16):
            pm = 16 * g + pmr
            refill(pm)
            bo = 0
            fo = 0
            T(lambda e, bo=bo: e.matmul(ps[1][:, :], kc["mvec_neg"][0:1, :],
                                        rstgb[0:1, bo:bo + 512],
                                        start=True, stop=False),
              waits=[DS, VE, SE], inc=None)
            T(lambda e, bo=bo: e.matmul(ps[1][:, :], kc["mvec_neg"][0:1, :],
                                        rstgb[0:1, bo + 512:bo + 1024],
                                        start=False, stop=False), inc=None)
            T(lambda e, fo=fo: e.matmul(ps[1][:, :], kc["row_quart_f"][0:1, :],
                                        rstgf[0:1, fo:fo + 512],
                                        start=False, stop=True))
            Sc(lambda e: e.activation(mgb, ps[1][:, :], AF.Exp), waits=[PE])
            V(lambda e, pmr=pmr: e.scalar_tensor_tensor(
                tabB[:, pmr * 512:(pmr + 1) * 512],
                mgb, float(np.sqrt(2.0)),
                tabB[:, pmr * 512:(pmr + 1) * 512], ALU.mult, ALU.mult), waits=[SE])

    def p_matmuls(g):
        tabB = rg["tabB"](); ztm = rg["ztm"](); PU = rg["PUr"]()
        for bank in range(4):
            pbank = ps[2 + bank]
            for s in range(16):
                tl = bank * 16 + s
                gl = tl
                T(lambda e, tl=tl, s=s, st=True: e.matmul(
                    pbank[0:64, s * 32:(s + 1) * 32],
                    tabB[:, tl * 128:tl * 128 + 64],
                    ztm[:].rearrange("p (c ch) -> p ch c", c=32)[:, tl, :],
                    start=True, stop=True), waits=[VE, SE, DQ], inc=None)
                T(lambda e, tl=tl, s=s: e.matmul(
                    pbank[64:128, s * 32:(s + 1) * 32],
                    tabB[:, tl * 128 + 64:tl * 128 + 128],
                    ztm[:].rearrange("p (c ch) -> p ch c", c=32)[:, 64 + tl, :],
                    start=True, stop=True, tile_position=(0, 64)))
            for pmr in range(4):
                def evac(e, bank=bank, pmr=pmr):
                    base = 16 * g + bank * 4 + pmr
                    return e.tensor_copy(
                        PU[:].rearrange("p (c cd pm) -> p c cd pm", c=32, cd=4)
                        [:, :, :, base].rearrange("p c cd -> p cd c"),
                        ps[2 + bank][:, :].rearrange("p (pm cd c) -> p pm cd c",
                                                     pm=4, cd=4)[:, pmr, :, :])
                V(evac, waits=[PE])

    def aprime(g):
        Dma("sync", lambda e: e.dma_start_transpose(
            out=rg["tabA1"]()[:].rearrange("p (tl j) -> p tl j", tl=64),
            in_=rg["tabB"]()[:].rearrange("p (tl c) -> p tl c", tl=64)),
            waits=[VE, SE, PE])
        Dma("gpsimd", lambda e, g=g: e.dma_start(
            out=hb["ap"][:, g * 8192:(g + 1) * 8192], in_=rg["tabA1"]()),
            waits=[DQ])

    def scan():
        PU = rg["PUr"]()
        shuf_mask = []
        for i in range(16):
            shuf_mask += [2 * i + 1, 2 * i]
        for c in range(NCH):
            pslice = PU[:, c * 512:(c + 1) * 512]
            V(lambda e: e.stream_shuffle(s_sw[:], s_state[:], shuf_mask),
              waits=[DG, PE])
            V(lambda e: e.tensor_mul(s_ta[:], s_state[:], Lt1[:]))
            V(lambda e: e.tensor_mul(s_tb[:], s_sw[:], Lt2[:]))
            V(lambda e: e.tensor_add(s_ta[:], s_ta[:], s_tb[:]))
            V(lambda e, pslice=pslice: e.scalar_tensor_tensor(
                s_tb[:], pslice, kc["sgn_scan"][:, 0:1], s_ta[:],
                ALU.mult, ALU.add))
            V(lambda e, pslice=pslice: e.tensor_copy(pslice, s_state[:]))
            V(lambda e: e.tensor_copy(s_state[:], s_tb[:]))

    def granule2(l, g, dbg_y=None):
        tap = (lambda nm, shp: dbgt(nm, shp, BF16)) if (l == 0 and g == 0) \
            else (lambda nm, shp: None)
        d_spfre = tap("spf_re", [128, 4096]); d_spfim = tap("spf_im", [128, 4096])
        d_kk0 = tap("kk0", [128, 128])
        d_khar = tap("khar", [128, 128]); d_khbr = tap("khbr", [128, 128])
        d_khai = tap("khai", [128, 128])
        d_Yre = tap("Y_re", [128, 4096]); d_Yim = tap("Y_im", [128, 4096])
        def tdma(d, src):
            if d is not None:
                Dma("sync", lambda e, d=d, src=src: e.dma_start(out=d, in_=src),
                    waits=[VE, PE, SE])
        tabA = rg["tabA2"]()
        Dma("scalar", lambda e, g=g: e.dma_start(
            out=tabA, in_=hb["ap"][:, g * 8192:(g + 1) * 8192]),
            waits=[VE, PE, SE])
        zg_load(g)
        ztm_build()
        ztm = rg["ztm"](); sp_re = rg["sp_re"](); sp_im = rg["sp_im"]()
        spT = rg["spT"](); ytm = rg["ytm"]()
        for ct in range(8):
            T(lambda e, ct=ct: e.matmul(
                ps[0][:, :], kc["fwd_re"][:], ztm[:, ct * 512:(ct + 1) * 512],
                start=True, stop=True), waits=[DQ, VE], inc=None)
            T(lambda e, ct=ct: e.matmul(
                ps[1][:, :], kc["fwd_im"][:], ztm[:, ct * 512:(ct + 1) * 512],
                start=True, stop=True))
            V(lambda e, ct=ct: e.tensor_copy(sp_re[:, ct * 512:(ct + 1) * 512],
                                             ps[0][:, :]), waits=[PE])
            V(lambda e, ct=ct: e.tensor_copy(sp_im[:, ct * 512:(ct + 1) * 512],
                                             ps[1][:, :]))
        import kernel as _K2
        tdma(d_spfre, sp_re); tdma(d_spfim, sp_im)
        if getattr(_K2, "SUBSTAGE", "") == "g2a":
            return
        for tl in range(64):
            T(lambda e, tl=tl: e.matmul(
                ps[2][:, 2 * tl:2 * tl + 2], tabA[:, tl * 128:(tl + 1) * 128],
                wkvec[:, (16 * g + tl // 4) * 8 + (tl % 4) * 2:
                      (16 * g + tl // 4) * 8 + (tl % 4) * 2 + 2],
                start=True, stop=True),
              waits=[DS, VE], inc=(PE if tl == 63 else None))
        V(lambda e: e.tensor_copy(
            kk[:].rearrange("p (ab t) -> p ab t", ab=2),
            ps[2][:, 0:128].rearrange("p (t ab) -> p ab t", ab=2)), waits=[PE])
        T(lambda e: e.matmul(ps[3][:, 0:128], kc["fwdR_re"][:], kk[:],
                             start=True, stop=True), waits=[VE], inc=None)
        T(lambda e: e.matmul(ps[4][:, 0:128], kc["fwdR_im"][:], kk[:],
                             start=True, stop=True), inc=None)
        T(lambda e, g=g: e.matmul(ps[2][:, 128:256], kc["row_one"][0:1, :],
                                  dskrow[0:1, g * 128:(g + 1) * 128],
                                  start=True, stop=True), inc=None)
        T(lambda e, g=g: e.matmul(ps[2][:, 256:384], kc["e0vec"][0:1, :],
                                  dskrow[0:1, g * 128:(g + 1) * 128],
                                  start=True, stop=True))
        V(lambda e: e.tensor_copy(khAre[:], ps[3][:, 0:128]), waits=[PE])
        V(lambda e: e.tensor_add(khAre[:], khAre[:], ps[2][:, 128:256]))
        V(lambda e: e.tensor_copy(khAim[:], ps[4][:, 0:128]))
        V(lambda e: e.memset(khAim[0:1, :], 0.0))
        V(lambda e: e.tensor_copy(khBre[:], khAre[:]))
        V(lambda e: e.tensor_copy(khBre[0:1, :], ps[4][0:1, 0:128]))
        V(lambda e: e.tensor_add(khBre[0:1, :], khBre[0:1, :],
                                 ps[2][0:1, 256:384]))
        V(lambda e: e.tensor_copy(khBim[:], khAim[:]))
        tdma(d_kk0, kk[:]); tdma(d_khar, khAre[:])
        tdma(d_khbr, khBre[:]); tdma(d_khai, khAim[:])
        if getattr(_K2, "SUBSTAGE", "") == "g2b":
            return
        def khb(t):
            return t[:].unsqueeze(1).broadcast_to([128, 32, 128])
        spv = lambda t: t[:].rearrange("p (c ch) -> p c ch", c=32)
        V(lambda e: e.tensor_mul(spv(spT), spv(sp_re), khb(khBim)))
        V(lambda e: e.tensor_mul(spv(ztm), spv(sp_re), khb(khAre)))
        V(lambda e: e.tensor_mul(spv(sp_re), spv(sp_im), khb(khAim)))
        V(lambda e: e.tensor_sub(spv(sp_re), spv(ztm), spv(sp_re)))
        V(lambda e: e.tensor_mul(spv(sp_im), spv(sp_im), khb(khBre)))
        V(lambda e: e.tensor_add(spv(sp_im), spv(spT), spv(sp_im)))
        tdma(d_Yre, sp_re); tdma(d_Yim, sp_im)
        if getattr(_K2, "SUBSTAGE", "") == "g2c":
            return
        PU = rg["PUr"]()
        for t8 in range(8):
            bank = ps[5 + (t8 % 2)]
            def mvap(sp, t8=t8):
                a = sp[:].rearrange("p (c hh gs) -> p gs hh c", hh=2, gs=64)
                return a[:, 8 * t8:8 * t8 + 8, :, :]
            T(lambda e, bank=bank, mvap=mvap: e.matmul(
                bank[:, :], kc["inv_re"][:], mvap(sp_re),
                start=True, stop=False), waits=[VE], inc=None)
            T(lambda e, bank=bank, mvap=mvap: e.matmul(
                bank[:, :], kc["inv_im"][:], mvap(sp_im),
                start=False, stop=True), inc=None)
            for s in range(8):
                tl = t8 * 8 + s
                pmq = 16 * g + tl // 4
                cdq = tl % 4
                pos = cdq * 128 + pmq
                T(lambda e, tl=tl, s=s, pos=pos: e.matmul(
                    ps[0][:, s * 64:s * 64 + 32],
                    tabA[:, tl * 128:(tl + 1) * 128][0:64, :],
                    PU[:].rearrange("p (c q) -> p q c", c=32)[0:64, pos, :],
                    start=True, stop=True), inc=None)
                T(lambda e, tl=tl, s=s, pos=pos: e.matmul(
                    ps[1][:, s * 64 + 32:s * 64 + 64],
                    tabA[:, tl * 128:(tl + 1) * 128][64:128, :],
                    PU[:].rearrange("p (c q) -> p q c", c=32)[64:128, pos, :],
                    start=True, stop=True, tile_position=(64, 0)),
                  inc=(PE if s == 7 else None))
            for hh in range(2):
                def yslice(t8=t8, hh=hh):
                    return ytm[:].rearrange("p (c ch) -> p ch c", c=32)[
                        :, hh * 64 + 8 * t8:hh * 64 + 8 * t8 + 8, :]
                def evc(e, bank=bank, hh=hh, yslice=yslice):
                    return e.tensor_copy(
                        yslice(),
                        bank[:, :].rearrange("p (gs hh2 c) -> p gs hh2 c",
                                             gs=8, hh2=2)[:, :, hh, :])
                V(evc, waits=[PE])
                def eva(e, hh=hh, yslice=yslice):
                    return e.tensor_add(
                        yslice(), yslice(),
                        ps[hh][:, :].rearrange("p (gs hh2 c) -> p gs hh2 c",
                                               gs=8, hh2=2)[:, :, hh, :])
                V(eva)
        if dbg_y is not None:
            Dma("sync", lambda e, d=dbg_y, g=g: e.dma_start(
                out=d.rearrange("p (g t) -> p g t", g=8)[:, g, :], in_=ytm),
                waits=[VE])
        if getattr(_K2, "SUBSTAGE", "") == "g2d":
            return
        yv = ytm
        Sc(lambda e: e.activation(spT, yv, AF.Square), waits=[VE])
        V(lambda e: e.tensor_mul(spT, spT, yv), waits=[SE])
        V(lambda e: e.scalar_tensor_tensor(spT, spT, GC3, yv, ALU.mult, ALU.add))
        Sc(lambda e: e.activation(spT, spT, AF.Tanh, scale=SQ2PI), waits=[VE])
        V(lambda e: e.tensor_scalar(spT, spT, 0.5, 0.5, ALU.mult, ALU.add),
          waits=[SE])
        V(lambda e: e.tensor_mul(yv, yv, spT))
        Dma("sync", lambda e: e.dma_start_transpose(
            out=sp_re[:].rearrange("p (c i) -> p c i", c=32),
            in_=ytm[:].rearrange("p (c i) -> p c i", c=32)), waits=[VE])
        Dma("gpsimd", lambda e, g=g: e.dma_start(
            out=hb["y"][:, g * 4096:(g + 1) * 4096], in_=sp_re), waits=[DQ])

    def w_out(l):
        wres = rg["wres"](); yhalf = rg["yhalf"]()
        d_yg = dbgt("ygel_l%d" % l, [128, 8 * 4096], BF16)
        if d_yg is not None:
            Dma("sync", lambda e, d=d_yg: e.dma_start(out=d, in_=hb["y"]),
                waits=[DG, VE, PE])
        Dma("sync", lambda e: e.dma_start(out=A1[:], in_=hb["h"]),
            waits=[VE, PE, SE, DG, DS])
        for k in range(8):
            Dma("gpsimd", lambda e, k=k: e.dma_start(
                out=wres[:, k * 2048:(k + 1) * 2048],
                in_=pr["w_out_t"][l].rearrange("(k p) o -> p k o", p=128)[:, k, :]),
                waits=[VE, PE])
        sigb = pool[0][:].bitcast(BF16)[:, 0:512]
        zpb = pool[1][:].bitcast(BF16)[:, 0:512]
        d_wres = dbgt("wres_l%d" % l, [128, 16384], BF16)
        if d_wres is not None:
            Dma("sync", lambda e, d=d_wres: e.dma_start(out=d, in_=wres),
                waits=[DG, VE, PE])
        d_yhalf = dbgt("yhalf_l%d" % l, [128, 16384], BF16)
        d_zpb = dbgt("zpb_l%d" % l, [128, 512], BF16)
        for th in range(2):
            Dma("scalar", lambda e, th=th: e.dma_start(
                out=yhalf,
                in_=hb["y"][:].rearrange("p (g c i) -> p g c i", g=8, c=32)
                [:, :, 16 * th:16 * th + 16, :]), waits=[VE, PE, DG])
            if d_yhalf is not None and th == 0:
                Dma("sync", lambda e, d=d_yhalf: e.dma_start(out=d, in_=yhalf),
                    waits=[DS])
            for op_ in range(8):
                for oct_ in range(2):
                    for side, bnk in ((0, 0), (1, 2)):
                        ocol = op_ * 128 + side * 1024
                        for tt8 in range(8):
                            ttl = oct_ * 8 + tt8
                            for k in range(8):
                                lastall = (side == 1 and tt8 == 7 and k == 7)
                                T(lambda e, k=k, ocol=ocol, ttl=ttl, bnk=bnk,
                                  tt8=tt8: e.matmul(
                                    ps[bnk + tt8 // 4][:, (tt8 % 4) * 128:
                                                       (tt8 % 4) * 128 + 128],
                                    wres[:, k * 2048 + ocol:k * 2048 + ocol + 128],
                                    yhalf[:, k * 2048 + ttl * 128:k * 2048 + ttl * 128 + 128]
                                    [:, ::-1],
                                    start=(k == 0), stop=(k == 7)),
                                  waits=[DS, DQ, VE, SE],
                                  inc=(PE if lastall else None))
                    for bi in range(2):
                        linb, gateb = ps[0 + bi], ps[2 + bi]
                        Sc(lambda e, linb=linb, gateb=gateb, op_=op_: e.activation(
                            sigb, gateb[:, :], AF.Tanh, scale=0.5,
                            bias=bo2_sb[:, 8 + op_:9 + op_]), waits=[PE])
                        V(lambda e: e.tensor_scalar(sigb, sigb, 0.5, 0.5,
                                                    ALU.mult, ALU.add), waits=[SE])
                        V(lambda e, linb=linb, op_=op_: e.scalar_tensor_tensor(
                            zpb, linb[:, :], bo_sb[:, op_:op_ + 1], sigb,
                            ALU.add, ALU.mult))
                        def resadd(e, bi=bi, oct_=oct_, th=th, op_=op_):
                            tt0 = th * 16 + oct_ * 8 + bi * 4
                            dst = A1[:].rearrange("p (tt cb i) -> p tt cb i",
                                                  tt=32, cb=8)[:, tt0:tt0 + 4, op_, :]
                            src = zpb.rearrange("p (tt i) -> p tt i", tt=4)
                            return e.tensor_add(dst, dst, src)
                        V(resadd)
                        if d_zpb is not None and th == 0 and op_ == 0 \
                                and oct_ == 0 and bi == 0:
                            Dma("sync", lambda e, d=d_zpb: e.dma_start(
                                out=d, in_=zpb), waits=[VE])
        if l == n_inner - 1:
            for hq in range(4):
                Dma("scalar", lambda e, hq=hq: e.dma_start(
                    out=A2[:, hq * 8192:(hq + 1) * 8192],
                    in_=hb["x"][:, hq * 8192:(hq + 1) * 8192]), waits=[VE, PE])
            for hq in range(4):
                V(lambda e, hq=hq: e.tensor_add(
                    A1[:, hq * 8192:(hq + 1) * 8192],
                    A1[:, hq * 8192:(hq + 1) * 8192],
                    A2[:, hq * 8192:(hq + 1) * 8192]), waits=[DS])
            Dma("sync", lambda e: e.dma_start(out=hb["x"], in_=A1[:]), waits=[VE])
        else:
            Dma("sync", lambda e: e.dma_start(out=hb["h"], in_=A1[:]), waits=[VE])

    import kernel as _K
    sub = getattr(_K, "SUBSTAGE", "all")
    for l in range(n_inner):
        layer_smalls(l)
        if sub == "smalls":
            return
        layer_ln(l)
        if sub == "ln":
            return
        d_tabB = dbgt("tabB_l%d" % l, [128, 8192], BF16) if l == 0 else None
        d_PU = dbgt("PU_l%d" % l, [128, 16384], BF16) if l == 0 else None
        d_U = dbgt("U_l%d" % l, [128, 16384], BF16) if l == 0 else None
        d_y = dbgt("y_l%d" % l, [128, 8 * 4096], BF16) if l == 0 else None
        d_ztm = dbgt("ztm_l%d" % l, [128, 4096], BF16) if l == 0 else None
        for g in range(8):
            zg_load(g)
            ztm_build()
            if d_ztm is not None and g == 0:
                Dma("sync", lambda e, d=d_ztm: e.dma_start(out=d, in_=rg["ztm"]()),
                    waits=[DQ, VE])
            btable_build(g)
            p_matmuls(g)
            if d_tabB is not None and g == 0:
                Dma("sync", lambda e, d=d_tabB: e.dma_start(out=d, in_=rg["tabB"]()),
                    waits=[VE, PE, SE])
            if sub == "pug0" and g == 0:
                Dma("sync", lambda e, d=d_PU: e.dma_start(out=d, in_=rg["PUr"]()),
                    waits=[VE, PE])
                return
            aprime(g)
            if sub == "g1one" and g == 0:
                return
        if sub == "g1":
            return
        if d_PU is not None:
            Dma("sync", lambda e, d=d_PU: e.dma_start(out=d, in_=rg["PUr"]()),
                waits=[VE, PE])
        scan()
        if d_U is not None:
            Dma("sync", lambda e, d=d_U: e.dma_start(out=d, in_=rg["PUr"]()),
                waits=[VE, PE])
        if sub == "scan":
            return
        for g in range(8):
            granule2(l, g, dbg_y=d_y)
            if sub.startswith("g2") and sub != "g2" and g == 0:
                return
        if sub == "g2":
            return
        w_out(l)
        d_h = dbgt("h_l%d" % l, [128, 8 * L], BF16)
        if d_h is not None:
            Dma("sync", lambda e, d=d_h: e.dma_start(
                out=d, in_=hb["x"] if l == n_inner - 1 else hb["h"]),
                waits=[DQ, VE])

    if do_ffn:
        emit_ffn(ctx)
    emit_output(ctx)


def emit_ffn(ctx):
    nc, P, kc, pr, ps, pool = ctx["nc"], ctx["P"], ctx["kc"], ctx["pr"], ctx["ps"], ctx["pool"]
    A1, A2, sq_t = ctx["A1"], ctx["A2"], ctx["sq_t"]
    sm = ctx["smalls"]; rg = ctx["regions"]; hb = ctx["hbm"]
    DQ, DS, DG = ctx["sems"]["DQ"], ctx["sems"]["DS"], ctx["sems"]["DG"]
    PE, VE, SE, GP = ctx["sems"]["PE"], ctx["sems"]["VE"], ctx["sems"]["SE"], ctx["sems"]["GP"]
    kbias = sm["kbias"]
    gub_sb, gub2_sb = sm["gub_sb"], sm["gub2_sb"]
    stg_sc, dnb_sb, rtr_sb = sm["stg_sc"], sm["dnb_sb"], sm["rtr_sb"]
    scT = sm["scT"]
    stg_st = sm["stg_st"]
    st_sum, st_rstd, st_ns2 = sm["st_sum"], sm["st_rstd"], sm["st_ns2"]
    lgts = pool[0]
    AXX = mybir.AxisListType.X

    def V(fn, waits=()):
        P.op("vector", fn, waits=waits, inc=VE)
    def Sc(fn, waits=()):
        P.op("scalar", fn, waits=waits, inc=SE)
    def T(fn, waits=(), inc=PE):
        P.op("tensor", fn, waits=waits, inc=inc)
    def Dma(q, fn, waits=()):
        ctr = {"sync": DQ, "scalar": DS, "gpsimd": DG}[q]
        P.op(q, fn, waits=waits, inc=ctr, k=16)

    Dma("sync", lambda e: e.dma_start(out=A1[:], in_=hb["x"]),
        waits=[VE, PE, SE, DS, DG])
    nmm = [0]
    def src_fn(tt, cb):
        return A1[:, tt * 1024 + cb * 128:tt * 1024 + cb * 128 + 128]
    for tt in range(32):
        for cb in range(8):
            last = (tt == 31 and cb == 7)
            st = (nmm[0] == 0)
            V(lambda e, tt=tt, cb=cb: e.tensor_mul(
                sq_t[:, 0:128], src_fn(tt, cb), src_fn(tt, cb)), waits=[PE, DQ])
            T(lambda e, tt=tt, cb=cb, st=st, last=last: e.matmul(
                ps[6][0:32, 0:128],
                kc["lnsel"][:].rearrange("p (v c) -> p v c", v=32)[:, tt, :],
                src_fn(tt, cb), start=st, stop=last), waits=[VE, DQ], inc=None)
            T(lambda e, tt=tt, cb=cb, st=st, last=last: e.matmul(
                ps[7][0:32, 0:128],
                kc["lnsel"][:].rearrange("p (v c) -> p v c", v=32)[:, tt, :],
                sq_t[:, 0:128], start=st, stop=last))
            nmm[0] += 1
    V(lambda e: e.tensor_scalar_mul(st_sum[0:32, :], ps[6][0:32, 0:128], 1.0 / H),
      waits=[PE])
    V(lambda e: e.tensor_mul(st_rstd[0:32, :], st_sum[0:32, :], st_sum[0:32, :]))
    V(lambda e: e.scalar_tensor_tensor(
        st_rstd[0:32, :], ps[7][0:32, 0:128], 1.0 / H, st_rstd[0:32, :],
        ALU.mult, ALU.subtract))
    Sc(lambda e: e.activation(st_rstd[0:32, :], st_rstd[0:32, :], AF.Ln,
                              bias=kbias[0:32, 0:1]), waits=[VE])
    Sc(lambda e: e.activation(st_rstd[0:32, :], st_rstd[0:32, :], AF.Exp, scale=-0.5))
    V(lambda e: e.scalar_tensor_tensor(
        st_ns2[0:32, :], st_sum[0:32, :], -1.0, st_rstd[0:32, :],
        ALU.mult, ALU.mult), waits=[SE])
    for tt in range(32):
        if tt % 8 == 0:
            Dma("scalar", lambda e, tt=tt: e.dma_start(
                out=stg_st[0:1, 0:1024], in_=st_rstd[tt:tt + 8, :]),
                waits=[VE, PE])
            Dma("scalar", lambda e, tt=tt: e.dma_start(
                out=stg_st[0:1, 1024:2048], in_=st_ns2[tt:tt + 8, :]))
        o = (tt % 8) * 128
        T(lambda e, o=o: e.matmul(
            ps[4][:, 0:128], kc["row_one_f"][0:1, :], stg_st[0:1, o:o + 128],
            start=True, stop=True), waits=[DS, VE], inc=None)
        T(lambda e, o=o: e.matmul(
            ps[5][:, 0:128], kc["row_one_f"][0:1, :],
            stg_st[0:1, 1024 + o:1024 + o + 128], start=True, stop=True))
        def bc(psb):
            return psb[:, 0:128].unsqueeze(1).broadcast_to([128, 8, 128])
        V(lambda e, tt=tt: e.tensor_mul(
            sq_t[:, 0:1024].rearrange("p (cb i) -> p cb i", cb=8),
            A1[:, tt * 1024:(tt + 1) * 1024].rearrange("p (cb i) -> p cb i", cb=8),
            bc(ps[4])), waits=[PE])
        V(lambda e, tt=tt: e.tensor_add(
            A1[:, tt * 1024:(tt + 1) * 1024].rearrange("p (cb i) -> p cb i", cb=8),
            sq_t[:, 0:1024].rearrange("p (cb i) -> p cb i", cb=8), bc(ps[5])))
    Dma("sync", lambda e: e.dma_start(out=hb["z"], in_=A1[:]), waits=[VE])
    d = ctx["dbgt"]("z2", [128, 8 * 4096], BF16)
    if d is not None:
        Dma("sync", lambda e, d=d: e.dma_start(out=d, in_=A1[:]), waits=[VE])

    Dma("gpsimd", lambda e: e.dma_start(
        out=rtr_sb[:].rearrange("p (cb e) -> p cb e", cb=8),
        in_=pr["router"].rearrange("(cb p) e -> p cb e", p=128)), waits=[VE, PE])
    Dma("gpsimd", lambda e: e.dma_start(out=dnb_sb[0:16, :], in_=pr["downb"]))
    V(lambda e: e.tensor_scalar_mul(dnb_sb[0:16, :], dnb_sb[0:16, :], ALPHA),
      waits=[DG])

    wgu = lambda: A1[:, 16384:24576]
    wdn4 = lambda: A1[:, 16384:32768]
    screp = lambda: A1[:, 28672:30720]

    for th in range(2):
        Dma("sync", lambda e, th=th: e.dma_start(
            out=A1[:, 0:16384], in_=hb["z"][:, th * 16384:(th + 1) * 16384]),
            waits=[VE, PE, SE, DG, DS])
        for ttl in range(16):
            for cb in range(8):
                T(lambda e, ttl=ttl, cb=cb: e.matmul(
                    ps[7][0:128, 0:16],
                    A1[:, ttl * 1024 + cb * 128:ttl * 1024 + cb * 128 + 128],
                    rtr_sb[:, cb * 16:(cb + 1) * 16],
                    start=(cb == 0), stop=(cb == 7)), waits=[DQ, DS, VE],
                  inc=(PE if cb == 7 else None))
            V(lambda e, ttl=ttl: e.tensor_copy(lgts[:, ttl * 16:(ttl + 1) * 16],
                                               ps[7][0:128, 0:16]), waits=[PE])
        lg3 = lgts[:, 0:256].rearrange("p (t e) -> p t e", t=16)
        m1, m2 = pool[1], pool[2]
        msk3 = pool[3][:, 0:256].rearrange("p (t e) -> p t e", t=16)
        msk23 = pool[4][:, 0:256].rearrange("p (t e) -> p t e", t=16)
        w1v = pool[5]
        def bce(v):
            return v[:, 0:16].unsqueeze(2).broadcast_to([128, 16, 16])
        V(lambda e: e.reduce_max(m1[:, 0:16], lg3, axis=AXX))
        V(lambda e: e.tensor_tensor(msk3, lg3, bce(m1), ALU.is_ge))
        V(lambda e: e.scalar_tensor_tensor(msk23, msk3, -1e30, lg3, ALU.mult, ALU.add))
        V(lambda e: e.reduce_max(m2[:, 0:16], msk23, axis=AXX))
        V(lambda e: e.tensor_sub(w1v[:, 0:16], m1[:, 0:16], m2[:, 0:16]))
        Sc(lambda e: e.activation(w1v[:, 0:16], w1v[:, 0:16], AF.Tanh, scale=0.5),
           waits=[VE])
        V(lambda e: e.tensor_scalar(w1v[:, 0:16], w1v[:, 0:16], 0.5, 0.5,
                                    ALU.mult, ALU.add), waits=[SE])
        V(lambda e: e.tensor_tensor(msk23, msk23, bce(m2), ALU.is_ge))
        scoresv = sq_t[:, 0:256].rearrange("p (t e) -> p t e", t=16)
        V(lambda e: e.tensor_mul(scoresv, msk3, bce(w1v)))
        V(lambda e: e.tensor_scalar(w1v[:, 0:16], w1v[:, 0:16], -1.0, 1.0,
                                    ALU.mult, ALU.add))
        V(lambda e: e.tensor_mul(msk23, msk23, bce(w1v)))
        V(lambda e: e.tensor_add(scoresv, scoresv, msk23))
        for ttl in range(16):
            T(lambda e, ttl=ttl: e.transpose(
                ps[7][:, 0:128].bitcast(BF16)[0:16, 0:128],
                sq_t[:, 0:256].rearrange("p (t e) -> p t e", t=16)[:, ttl, :],
                kc["identb"][:]), waits=[VE])
            V(lambda e, ttl=ttl: e.tensor_scalar_mul(
                scT[0:16, ttl * 128:(ttl + 1) * 128],
                ps[7][:, 0:128].bitcast(BF16)[0:16, 0:128],
                1.0 / ALPHA), waits=[PE])
        d = ctx["dbgt"]("scores_th%d" % th, [128, 256], BF16)
        if d is not None:
            Dma("sync", lambda e, d=d: e.dma_start(out=d, in_=sq_t[:, 0:256]),
                waits=[VE])
        for eg in range(4):
            for ei in range(4):
                e_ = eg * 4 + ei
                Dma("gpsimd", lambda e, e_=e_: e.dma_start(
                    out=wgu().rearrange("p (k f) -> p k f", k=8),
                    in_=pr["gu"][e_].rearrange("(k p) f -> p k f", p=128)),
                    waits=[VE, PE])
                Dma("scalar", lambda e, e_=e_: e.dma_start(
                    out=gub_sb[:, 0:8], in_=pr["gub"][e_]), waits=[VE, PE])
                V(lambda e: e.tensor_scalar_mul(gub2_sb[:, 0:4], gub_sb[:, 0:4],
                                                ALPHA), waits=[DS])
                Dma("sync", lambda e, e_=e_: e.dma_start(
                    out=stg_sc[0:1, :], in_=scT[e_:e_ + 1, :]), waits=[VE, PE])
                for c4 in range(4):
                    T(lambda e, c4=c4: e.matmul(
                        ps[6][:, :], kc["row_one"][0:1, :],
                        stg_sc[0:1, c4 * 512:(c4 + 1) * 512],
                        start=True, stop=True), waits=[DQ, VE], inc=PE)
                    V(lambda e, c4=c4: e.tensor_copy(
                        screp()[:, c4 * 512:(c4 + 1) * 512], ps[6][:, :]),
                      waits=[PE])
                for gp in range(4):
                    for side, bnk in ((0, 0), (1, 4)):
                        gucol = side * 512 + gp * 128
                        for k in range(8):
                            st = (k == 0); last = (k == 7)
                            for t4 in range(4):
                                lastall = (side == 1 and k == 7 and t4 == 3)
                                T(lambda e, k=k, gucol=gucol, t4=t4, bnk=bnk,
                                  st=st, last=last: e.matmul(
                                    ps[bnk + t4][:, :],
                                    wgu()[:, k * 1024 + gucol:k * 1024 + gucol + 128],
                                    A1[:, 0:16384].rearrange(
                                        "p (tt cb i) -> p tt cb i", tt=16, cb=8)
                                    [:, t4 * 4:t4 * 4 + 4, k, :],
                                    start=st, stop=last),
                                  waits=[DG, DQ, VE, SE],
                                  inc=(PE if lastall else None))
                    for t4 in range(4):
                        upb, gtb = ps[4 + t4], ps[0 + t4]
                        up1 = pool[1][:].bitcast(BF16)[:, 0:512]
                        sl1 = pool[2][:].bitcast(BF16)[:, 0:512]
                        Sc(lambda e, upb=upb, gp=gp: e.activation(
                            up1, upb[:, :], AF.Identity,
                            bias=gub_sb[:, 4 + gp:5 + gp]), waits=[PE])
                        Sc(lambda e, gtb=gtb, gp=gp: e.activation(
                            sl1, gtb[:, :], AF.Identity,
                            bias=gub_sb[:, gp:gp + 1]))
                        V(lambda e: e.tensor_scalar_min(sl1, sl1, LIMIT), waits=[SE])
                        Sc(lambda e: e.activation(sl1, sl1, AF.Silu, scale=ALPHA),
                           waits=[VE])
                        V(lambda e: e.tensor_scalar(up1, up1, LIMIT, -LIMIT,
                                                    ALU.min, ALU.max), waits=[SE])
                        V(lambda e: e.tensor_scalar_add(up1, up1, 1.0))
                        V(lambda e: e.tensor_mul(up1, up1, sl1))
                        def actw(e, ei=ei, gp=gp, t4=t4):
                            dst = A2[:].rearrange("p (e2 kq t) -> p e2 kq t",
                                                  e2=4, kq=4)[:, ei, gp,
                                                              t4 * 512:(t4 + 1) * 512]
                            return e.tensor_mul(dst, up1,
                                                screp()[:, t4 * 512:(t4 + 1) * 512])
                        V(actw)
            for ei in range(4):
                e_ = eg * 4 + ei
                Dma("gpsimd", lambda e, e_=e_, ei=ei: e.dma_start(
                    out=wdn4()[:, ei * 4096:(ei + 1) * 4096]
                        .rearrange("p (kq h2) -> p kq h2", kq=4),
                    in_=pr["down"][e_].rearrange("(kq p) h2 -> p kq h2", p=128)),
                    waits=[VE, PE, SE])
            for hc in range(8):
                for t4 in range(4):
                    first = True
                    for ei in range(4):
                        for kq in range(4):
                            st = first; first = False
                            last = (ei == 3 and kq == 3)
                            lastall = (t4 == 3 and last and eg != 0)
                            T(lambda e, hc=hc, t4=t4, kq=kq, ei=ei, st=st,
                              last=last: e.matmul(
                                ps[t4][:, :],
                                wdn4()[:, ei * 4096 + kq * 1024 + hc * 128:
                                       ei * 4096 + kq * 1024 + hc * 128 + 128],
                                A2[:].rearrange("p (e2 kq t) -> p e2 kq t",
                                                e2=4, kq=4)
                                [:, ei, kq, t4 * 512:(t4 + 1) * 512],
                                start=st, stop=last),
                              waits=[DG, VE],
                              inc=(PE if lastall else None))
                    if eg == 0:
                        T(lambda e, hc=hc, t4=t4: e.matmul(
                            ps[4 + t4][:, :], dnb_sb[0:16, hc * 128:(hc + 1) * 128],
                            scT[0:16, t4 * 512:(t4 + 1) * 512],
                            start=True, stop=True), waits=[DS],
                          inc=(PE if t4 == 3 else None))
                for t4 in range(4):
                    V(lambda e, t4=t4: e.tensor_copy(pool[1][:], ps[t4][:, :]),
                      waits=[PE])
                    if eg == 0:
                        V(lambda e, t4=t4: e.tensor_add(pool[1][:], pool[1][:],
                                                        ps[4 + t4][:, :]))
                    def accdma(e, eg=eg, th=th, hc=hc, t4=t4):
                        dst = hb["acc"].bitcast(F32)[:, hc * 4096 + th * 2048 +
                                                     t4 * 512:
                                                     hc * 4096 + th * 2048 +
                                                     t4 * 512 + 512]
                        if eg == 0:
                            return e.dma_start(out=dst, in_=pool[1][:])
                        return e.dma_start(out=dst, in_=pool[1][:],
                                           accum_op=ALU.add)
                    Dma("gpsimd", accdma, waits=[VE])


def emit_output(ctx):
    nc, P, kc, pr, ps, pool = ctx["nc"], ctx["P"], ctx["kc"], ctx["pr"], ctx["ps"], ctx["pool"]
    A1, A2 = ctx["A1"], ctx["A2"]
    hb = ctx["hbm"]
    DQ, DS, DG = ctx["sems"]["DQ"], ctx["sems"]["DS"], ctx["sems"]["DG"]
    PE, VE, SE = ctx["sems"]["PE"], ctx["sems"]["VE"], ctx["sems"]["SE"]
    out_t = ctx["out_t"]
    do_ffn = ctx["do_ffn"]

    def V(fn, waits=()):
        P.op("vector", fn, waits=waits, inc=VE)
    def Dma(q, fn, waits=()):
        ctr = {"sync": DQ, "scalar": DS, "gpsimd": DG}[q]
        P.op(q, fn, waits=waits, inc=ctr, k=16)

    Dma("sync", lambda e: e.dma_start(out=A1[:], in_=hb["x"]),
        waits=[VE, PE, SE, DG, DS])
    if do_ffn:
        for cb in range(8):
            slot = (cb % 2) * 8192
            Dma("scalar", lambda e, cb=cb, slot=slot: e.dma_start(
                out=A2[:, slot:slot + 8192].bitcast(F32),
                in_=hb["acc"][:, cb * 4096:(cb + 1) * 4096]),
                waits=[VE, DG])
            V(lambda e, cb=cb, slot=slot: e.tensor_add(
                A1[:].rearrange("p (tt cb2 i) -> p tt cb2 i", tt=32, cb2=8)
                [:, :, cb, :],
                A1[:].rearrange("p (tt cb2 i) -> p tt cb2 i", tt=32, cb2=8)
                [:, :, cb, :],
                A2[:, slot:slot + 8192].bitcast(F32)
                .rearrange("p (tt i) -> p tt i", tt=32)), waits=[DS, DQ])
    for tt in range(32):
        Dma("sync", lambda e, tt=tt: e.dma_start_transpose(
            out=A2[:, tt * 1024:(tt + 1) * 1024].rearrange("p (cb c) -> p cb c", cb=8),
            in_=A1[:, tt * 1024:(tt + 1) * 1024].rearrange("p (cb i) -> p cb i", cb=8)),
            waits=[VE])
    for part in range(4):
        Dma("gpsimd", lambda e, p_=part: e.dma_start(
            out=out_t.rearrange("(tt tp) c -> tp tt c", tp=128)
            [:, p_ * 8:(p_ + 1) * 8, :],
            in_=A2[:, p_ * 8192:(p_ + 1) * 8192].rearrange("p (tt c) -> p tt c", tt=8)),
            waits=[DQ])


def kernel(**inputs):
    from concourse.bass_utils import run_bass_kernel_spmd

    hp = host_prepare(inputs)
    x_full = np.asarray(inputs["L0_attn_x"], np.float32)
    nc, _ = build_program(dbg=(), n_inner=NL, do_ffn=True)

    shared = {
        "log_dt_s": hp["log_dt_s"], "A_re_s": hp["A_re_s"],
        "A_im_s": hp["A_im_s"], "C_re_s": hp["C_re_s"], "C_im_s": hp["C_im_s"],
        "lnwb": hp["lnwb"], "dskg": hp["dskg"],
        "w_out_t": hp["w_out_t"], "b_out": hp["b_out"],
        "router": np.ascontiguousarray(inputs["L0_ffn_router_weight"]),
        "gu": hp["gu"], "gub": hp["gub"],
        "down": np.ascontiguousarray(inputs["L0_ffn_down_proj"]),
        "downb": np.ascontiguousarray(inputs["L0_ffn_down_proj_bias"]),
    }
    for name, arr in _CONSTS.items():
        shared["c_" + name] = np.ascontiguousarray(arr)

    in_maps = []
    for core in range(8):
        m = dict(shared)
        m["x"] = np.ascontiguousarray(x_full[core])
        in_maps.append(m)

    res = run_bass_kernel_spmd(nc, in_maps, core_ids=list(range(8)))
    out = np.stack([np.asarray(res.results[i]["out"], np.float32)
                    for i in range(8)], axis=0)
    return out
